# version 10
# baseline (speedup 1.0000x reference)
import numpy as np
from contextlib import ExitStack

import concourse.bass as bass
import concourse.mybir as mybir
from concourse.bass_utils import run_bass_kernel_spmd

F32 = mybir.dt.float32
BF16 = mybir.dt.bfloat16
U8 = mybir.dt.uint8
AF = mybir.ActivationFunctionType
ALU = mybir.AluOpType
AX = mybir.AxisListType

B, T, D, H, HD, LAT = 16, 2048, 1024, 8, 256, 64
NCORES = 8
S = 2
C = 8
TTN = 4
TW = 512
NE = 9
NG = 4
EPS = 1e-5
BIG = 1.0e5
KB = 1024


PARANOID = [False]
DRAIN_ALL = [False]


class Plan:
    ENGS = ("tensor", "vector", "scalar", "gpsimd", "sync")

    def __init__(self):
        self.q = {e: [] for e in self.ENGS}
        self.cnt = {}
        self.last_wait = {e: {} for e in self.ENGS}

    def sem(self, name):
        self.cnt[name] = 0

    def now(self, name):
        return self.cnt[name]

    def emit(self, eng, fn, waits=(), inc=None, inc_n=1):
        lw = self.last_wait[eng]
        eff = []
        if PARANOID[0]:
            waits = list(waits) + [(k, self.cnt[k]) for k in self.cnt]
        for s_, v in waits:
            if v <= 0 or lw.get(s_, -1) >= v:
                continue
            lw[s_] = v
            eff.append((s_, v))
        if inc is not None:
            self.cnt[inc] += inc_n

        da = DRAIN_ALL[0] and eng in ("vector", "scalar", "gpsimd")

        def run(e, sems, _eff=tuple(eff), _fn=fn, _inc=inc, _n=inc_n, _da=da):
            for s_, v in _eff:
                e.wait_ge(sems[s_], v)
            if _da:
                e.drain()
            inst = _fn(e, sems)
            if _inc is not None:
                inst.then_inc(sems[_inc], _n)
        self.q[eng].append(run)
        return self.cnt[inc] if inc is not None else None


def build_program(stage="full", ffn_experts=None):
    nc = bass.Bass("TRN2", target_bir_lowering=False, debug=False)

    x_img = nc.dram_tensor("x_img", [S, TTN, 128, C * TW], F32, kind="ExternalInput").ap()
    qo_pack = nc.dram_tensor("qo_pack", [H, 2, 128, 2048], F32, kind="ExternalInput").ap()
    kv_pack = nc.dram_tensor("kv_pack", [H, 64, 512], F32, kind="ExternalInput").ap()
    wr_pack = nc.dram_tensor("wr_pack", [128, 576], F32, kind="ExternalInput").ap()
    ffn_pack = nc.dram_tensor("ffn_pack", [NE, NG, 6, 128, 2048], F32, kind="ExternalInput").ap()
    ident_in = nc.dram_tensor("ident", [128, 128], F32, kind="ExternalInput").ap()
    sel8_in = nc.dram_tensor("sel8", [8, 1024], F32, kind="ExternalInput").ap()
    out_img = nc.dram_tensor("out_img", [S, TTN, 128, C * TW], F32, kind="ExternalOutput").ap()
    snaps = nc.dram_tensor("snaps", [16, 128, C * TW], F32, kind="ExternalOutput").ap() \
        if stage in ("snap", "snapxg") else None

    al = nc.alloc_sbuf_tensor
    x_run = al("x_run", [128, C * 2048], F32)
    xlnT = al("xlnT", [128, C * 2048], BF16)
    lg = al("lg", [128, 4 * KB], U8)
    latT = lg.bitcast(BF16)[0:64, 0:2048]
    gatesT = lg.bitcast(BF16)[0:8, 0:2048]

    slot = al("slot", [128, 30 * KB], U8)
    slot_b = slot.bitcast(BF16)
    slot_f = slot.bitcast(F32)
    kT_sb = slot_b[:, 0:4096]
    V_sb = slot_b[:, 4096:8192]
    qT_sb = slot_b[:, 8192:9216]
    attnT = [slot_b[:, 9216 + i * 512: 9216 + (i + 1) * 512] for i in range(4)]
    oT_sb = slot_b[:, 11264:12288]
    rb_sb = slot_f[:, 6144:6656]
    qsq_sb = slot_f[:, 6656:7168]
    ksq_sb = slot_f[:, 7168:7680]
    x_ring = slot_f[:, 0:4096]
    xln2f = slot_f[:, 0:4096]
    xg_sb = slot_b[:, 0:4096]
    h_sb = slot_b[:, 4096:6144]
    fftmp = [slot_f[:, 6144 + i * 512: 6144 + (i + 1) * 512] for i in range(2)]
    gb_sb = slot_b[:, 10240:10752]

    war = al("war", [128, 56 * KB], U8)
    war_f = war.bitcast(F32)
    war_b = war.bitcast(BF16)
    stage_f2 = [war_f[:, 0:2048], war_f[:, 2048:4096]]
    win_r = {0: war_b[:, 8192:12288], 1: war_b[:, 16384:20480]}
    wsw_r = {0: war_b[:, 12288:16384], 1: war_b[:, 20480:24576]}
    wout_r = war_b[:, 24576:28672]
    qo_bf = {0: war_b[:, 16384:20480], 1: war_b[:, 20480:24576]}
    REG_WIN = {0: "A_lo", 1: "B_lo"}
    REG_WSW = {0: "A_hi", 1: "B_hi"}
    REG_QO = {0: "B_lo", 1: "B_hi"}

    kvbuf = al("kvbuf", [64, 1024], BF16)
    wr_f = al("wr_f", [128, 576], F32)
    wkvd_bf = al("wkvd_bf", [128, 512], BF16)
    ident_f = al("ident_f", [128, 128], F32)
    sel_sb = al("sel_sb", [8, 1024], BF16)
    rows = al("rows", [1, 2564], F32)
    r_srow = rows[0:1, 0:512]
    r_ssrow = rows[0:1, 512:1024]
    r_mrow = rows[0:1, 1024:1536]
    r_m2row = rows[0:1, 1536:2048]
    r_ksq = rows[0:1, 0:2048]
    r_rstd = rows[0:1, 2048:2560]
    r_negq = rows[0:1, 2560:2561]
    r_negk = rows[0:1, 2561:2562]
    r_negs = rows[0:1, 2562:2563]
    ones_f = al("ones_f", [128, 1], F32)
    ones_b = al("ones_b", [128, 1], BF16)
    crow_1 = al("crow_1", [1, 128], F32)
    crow_1b = al("crow_1b", [1, 128], BF16)
    crow_n1024 = al("crow_n1024", [1, 128], F32)
    crow_1_32 = al("crow_1_32", [1, 128], F32)
    bias_sb = al("bias_sb", [128, 2], F32)
    sc_sb = al("sc_sb", [128, 8], F32)
    sg_sb = al("sg_sb", [128, 8], F32)
    ind_sb = al("ind_sb", [128, 8], F32)
    sx_sb = al("sx_sb", [128, 8], F32)
    gtm_sb = al("gtm_sb", [128, 8], F32)
    m1_sb = al("m1_sb", [128, 1], F32)
    m2_sb = al("m2_sb", [128, 1], F32)
    rrow_sb = rows[0:1, 2048:2560]

    pairs = [nc.alloc_psum_tensor("pair%d" % i, [128, 1024], F32) for i in range(4)]
    bank = [pairs[i // 2][:, (i % 2) * 512:(i % 2) * 512 + 512] for i in range(8)]

    P = Plan()
    for s_ in ("WL", "WL0", "WL1", "XL", "ST", "PE", "DVE", "ACT", "PL", "SN"):
        P.sem(s_)
    snap_i = [0]

    def snap_buf(src_ap2, dst_rearr=True):
        if snap_i[0] >= 16:
            return
        i = snap_i[0]
        snap_i[0] += 1
        dst_ap = snaps[i][0:src_ap2.shape[0], 0:src_ap2.shape[1]]
        P.emit("sync", lambda e, S_, s2=src_ap2, d2=dst_ap: e.dma_start(d2, s2),
               waits=[("DVE", P.now("DVE"))], inc="SN", inc_n=16)

    def snap_tile0(tag):
        if stage != "snap" or snap_i[0] >= 16:
            return
        i = snap_i[0]
        snap_i[0] += 1
        src_ap = x_run[:].rearrange("p (c q) -> p c q", c=C)[:, :, 0:512]
        dst_ap = snaps[i].rearrange("p (c q) -> p c q", c=C)
        P.emit("sync", lambda e, S_, s2=src_ap, d2=dst_ap: e.dma_start(d2, s2),
               waits=[("DVE", P.now("DVE"))], inc="SN", inc_n=16)
    wp = {}
    region_pe = {}

    def use_region(name):
        region_pe[name] = P.now("PE")

    def dma(out_ap, in_ap, sem, waits=()):
        return P.emit("sync", lambda e, S_, o=out_ap, i=in_ap: e.dma_start(o, i),
                      waits=waits, inc=sem, inc_n=16)

    def mm(out_ap, lhsT, rhs, start, stop, waits=(), force_inc=False):
        return P.emit("tensor",
                      lambda e, S_, o=out_ap, l=lhsT, r=rhs, st=start, sp=stop:
                      e.matmul(o, l, r, start=st, stop=sp),
                      waits=waits, inc="PE" if (stop or force_inc) else None)

    def v_op(fn, waits=(), inc=None):
        return P.emit("vector", fn, waits=waits, inc=inc)

    def a_op(fn, waits=(), inc=None):
        return P.emit("scalar", fn, waits=waits, inc=inc)

    def v_drain():
        P.emit("vector", lambda e, S_: e.drain())

    st_state = {"i": 0, "cast": {0: 0, 1: 0}}

    def stream_unit(src_ap, dest_ap, regions, nrow=128):
        par = st_state["i"] % 2
        st_state["i"] += 1
        wsem = "WL%d" % par
        st = stage_f2[par] if nrow == 128 else stage_f2[par][0:nrow, 0:512]
        dv = dma(st, src_ap, wsem, waits=[("PL", st_state["cast"][par])])
        waits = [(wsem, dv)] + [("PE", region_pe.get(r, 0)) for r in regions]
        cv = P.emit("gpsimd", lambda e, S_, o=dest_ap, i=st: e.tensor_copy(o, i),
                    waits=waits, inc="PL")
        st_state["cast"][par] = cv
        return cv

    v_op(lambda e, S_: e.memset(ones_f[:], 1.0), inc="DVE")
    v_op(lambda e, S_: e.memset(ones_b[:], 1.0), inc="DVE")
    v_op(lambda e, S_: e.memset(crow_1[:], 1.0), inc="DVE")
    v_op(lambda e, S_: e.memset(crow_1b[:], 1.0), inc="DVE")
    v_op(lambda e, S_: e.memset(crow_n1024[:], -1.0 / 1024.0), inc="DVE")
    v_op(lambda e, S_: e.memset(crow_1_32[:], 1.0 / 32.0), inc="DVE")
    consts_done = P.now("DVE")
    dma(ident_f[:], ident_in, "WL")
    dma(wr_f[:], wr_pack, "WL")
    sel_f = stage_f2[0][0:8, 0:1024]
    preload_ld = dma(sel_f, sel8_in, "WL")
    ident_ld = wr_ld = sel_ld0 = preload_ld
    sel_ld = v_op(lambda e, S_: e.tensor_copy(sel_sb[:], sel_f),
                  waits=[("WL", sel_ld0)], inc="DVE")
    sel_pl = P.emit("gpsimd", lambda e, S_: e.engine_nop(), waits=[("DVE", sel_ld)], inc="PL")
    st_state["cast"][0] = sel_pl
    wkvd_done = v_op(lambda e, S_: e.tensor_copy(wkvd_bf[:], wr_f[:, 0:512]),
                     waits=[("WL", wr_ld)], inc="DVE")

    def ln_tiles(s, kind):
        for tt in range(TTN):
            if kind == "hbm":
                w = [("DVE", wp.get(("xring_free_dve", s), 0)),
                     ("PE", wp.get(("xring_free_pe", s), 0))]
                if tt > 0:
                    w.append(("DVE", wp[("xring_consumed", s, tt - 1)]))
                if stage in ("snap", "snapxg"):
                    w.append(("SN", P.now("SN")))
                ldv = dma(x_ring, x_img[s, tt], "XL", waits=w)
                xs = [x_ring[:, c * 512:(c + 1) * 512] for c in range(C)]
                base = [("XL", ldv), ("DVE", wp.get(("seq_prev_dve", s), 0))]
            else:
                xs = [x_run[:, c * 2048 + tt * 512: c * 2048 + (tt + 1) * 512] for c in range(C)]
                base = [("DVE", wp[("xmid_done", s)]), ("PE", wp[("attn_done_pe", s)])]

            for c in range(C):
                waits = list(base)
                if c == 0:
                    waits += [("DVE", consts_done),
                              ("DVE", wp.get(("ln_apply", s, kind, tt - 1), 0)),
                              ("DVE", wp.get(("lat_evac", s, tt - 1), 0) if kind == "hbm" else
                               wp.get(("router_done", s, tt - 1), 0))]
                mm(bank[4][0:1, :], ones_f[:], xs[c], c == 0, c == C - 1, waits=waits)
            sum_pe = P.now("PE")
            for c in range(C):
                sq = qsq_sb if c % 2 == 0 else ksq_sb
                aw = list(base) + [("PE", wp.get(("ln_sq_used", s, kind, tt, c - 2), 0))]
                av = a_op(lambda e, S_, o=sq, i=xs[c]: e.activation(o, i, AF.Square),
                          waits=aw, inc="ACT")
                mm(bank[5][0:1, :], ones_f[:], sq, c == 0, c == C - 1, waits=[("ACT", av)],
                   force_inc=True)
                wp[("ln_sq_used", s, kind, tt, c)] = P.now("PE")
            ssum_pe = P.now("PE")

            v_op(lambda e, S_: e.tensor_copy(r_srow, bank[4][0:1, :]),
                 waits=[("PE", sum_pe)], inc="DVE")
            v_op(lambda e, S_: e.tensor_copy(r_ssrow, bank[5][0:1, :]),
                 waits=[("PE", ssum_pe)], inc="DVE")
            v_drain()
            v_op(lambda e, S_: e.tensor_scalar(r_mrow, r_srow, 1.0 / 1024.0, None, op0=ALU.mult))
            v_drain()
            v_op(lambda e, S_: e.tensor_tensor(r_m2row, r_mrow, r_mrow, op=ALU.mult))
            v_op(lambda e, S_: e.tensor_scalar(r_ssrow, r_ssrow, 1.0 / 1024.0, None, op0=ALU.mult))
            v_drain()
            v_op(lambda e, S_: e.tensor_tensor(r_ssrow, r_ssrow, r_m2row, op=ALU.subtract))
            v_drain()
            var_done = v_op(lambda e, S_: e.tensor_scalar(r_ssrow, r_ssrow, EPS, None, op0=ALU.add),
                            inc="DVE")
            lnv = a_op(lambda e, S_: e.activation(r_rstd, r_ssrow, AF.Ln),
                       waits=[("DVE", var_done), ("PE", wp.get(("rs_mm", s), 0))], inc="ACT")
            P.emit("scalar", lambda e, S_: e.drain())
            rstd_done = a_op(lambda e, S_: e.activation(r_rstd, r_rstd, AF.Exp, scale=-0.5),
                             waits=[("ACT", lnv)], inc="ACT")
            srow_dve = P.now("DVE")
            mm(bank[6][:, :], crow_n1024[:], r_srow, True, True, waits=[("DVE", srow_dve)])
            rs = mm(bank[7][:, :], crow_1[:], r_rstd, True, True, waits=[("ACT", rstd_done)])
            wp[("rs_mm", s)] = rs
            def _emit_t(c):
                w0 = [("PE", rs)]
                if kind == "xrun" and c == 0:
                    w0.append(("PE", wp.get(("router_pe", s, tt - 1), 0)))
                v_op(lambda e, S_, x_=xs[c], f_=fftmp[c % 2]: e.tensor_tensor(
                    f_, x_, bank[6][:, :], op=ALU.add), waits=w0)

            def _emit_out(c):
                dst = xlnT[:, c * 2048 + tt * 512: c * 2048 + (tt + 1) * 512]
                v_op(lambda e, S_, d=dst, f_=fftmp[c % 2]: e.tensor_tensor(
                    d, f_, bank[7][:, :], op=ALU.mult), inc="DVE")
                if kind == "xrun":
                    v_op(lambda e, S_, c_=c, f_=fftmp[c % 2]: e.tensor_tensor(
                        xln2f[:, c_ * 512:(c_ + 1) * 512], f_, bank[7][:, :], op=ALU.mult),
                        inc="DVE")

            _emit_t(0)
            for c in range(1, C):
                _emit_t(c)
                _emit_out(c - 1)
            _emit_out(C - 1)
            wp[("ln_apply", s, kind, tt)] = P.now("DVE")
            if kind == "hbm":
                wp[("xring_consumed", s, tt)] = P.now("DVE")
                for c in range(C):
                    waits = [("DVE", wp[("ln_apply", s, "hbm", tt)])]
                    if c == 0:
                        waits += [("DVE", wkvd_done), ("DVE", wp.get(("lat_evac", s, tt - 1), 0))]
                    mm(bank[5][0:64, :], wkvd_bf[:, c * 64:(c + 1) * 64],
                       xlnT[:, c * 2048 + tt * 512: c * 2048 + (tt + 1) * 512],
                       c == 0, c == C - 1, waits=waits)
                u = P.now("PE")
                ev = v_op(lambda e, S_, tt_=tt: e.tensor_copy(
                    latT[:, tt_ * 512:(tt_ + 1) * 512], bank[5][0:64, :]),
                    waits=[("PE", u)], inc="DVE")
                wp[("lat_evac", s, tt)] = ev
            else:
                router_tile(s, tt)

    def router_tile(s, tt):
        for tc in range(4):
            for c in range(C):
                waits = [("DVE", wp[("ln_apply", s, "xrun", tt)])]
                if c == 0:
                    waits += [("WL", wr_ld), ("DVE", wp.get(("sc_evac", s, tt, tc - 1), 0))]
                mm(bank[4][:, 0:8], xln2f[:, c * 512 + tc * 128: c * 512 + (tc + 1) * 128],
                   wr_f[:, 512 + c * 8: 512 + (c + 1) * 8], c == 0, c == C - 1, waits=waits)
            u = P.now("PE")
            sc = v_op(lambda e, S_: e.tensor_copy(sc_sb[:], bank[4][:, 0:8]),
                      waits=[("PE", u)], inc="DVE")
            wp[("sc_evac", s, tt, tc)] = sc
            v_drain()
            v_op(lambda e, S_: e.tensor_reduce(m1_sb[:], sc_sb[:], axis=AX.X, op=ALU.max))
            v_drain()
            v_op(lambda e, S_: e.tensor_scalar(ind_sb[:], sc_sb[:], m1_sb[:], None, op0=ALU.is_ge))
            v_drain()
            v_op(lambda e, S_: e.scalar_tensor_tensor(sx_sb[:], ind_sb[:], -BIG, sc_sb[:],
                                                      op0=ALU.mult, op1=ALU.add))
            v_drain()
            v_op(lambda e, S_: e.tensor_reduce(m2_sb[:], sx_sb[:], axis=AX.X, op=ALU.max))
            v_drain()
            v_op(lambda e, S_: e.tensor_scalar(ind_sb[:], sc_sb[:], m2_sb[:], None, op0=ALU.is_ge),
                 inc="DVE")
            av = a_op(lambda e, S_: e.activation(sg_sb[:], sc_sb[:], AF.Exp, scale=-1.0),
                      waits=[("DVE", sc)], inc="ACT")
            v_op(lambda e, S_: e.tensor_scalar(sg_sb[:], sg_sb[:], 1.0, None, op0=ALU.add),
                 waits=[("ACT", av)])
            v_drain()
            v_op(lambda e, S_: e.reciprocal(sg_sb[:], sg_sb[:]))
            v_drain()
            gv = v_op(lambda e, S_: e.tensor_tensor(gtm_sb[:], sg_sb[:], ind_sb[:], op=ALU.mult),
                      inc="DVE")
            tr = P.emit("tensor",
                        lambda e, S_: e.transpose(bank[5][0:8, 0:128], gtm_sb[:], ident_f[:]),
                        waits=[("DVE", gv), ("WL", ident_ld),
                               ("DVE", wp.get(("gT_evac", s, tt, tc - 1), 0))], inc="PE")
            ge = v_op(lambda e, S_, tc_=tc: e.tensor_copy(
                gatesT[0:8, tt * 512 + tc_ * 128: tt * 512 + (tc_ + 1) * 128],
                bank[5][0:8, 0:128]),
                waits=[("PE", tr)], inc="DVE")
            wp[("gT_evac", s, tt, tc)] = ge
        wp[("router_pe", s, tt)] = P.now("PE")
        wp[("router_done", s, tt)] = P.now("DVE")

    def stream_head(s, h):
        par = h % 2
        cv = None
        for u in range(2):
            cv = stream_unit(qo_pack[h, u], qo_bf[par][:, u * 2048:(u + 1) * 2048],
                             [REG_QO[par]])
        wp[("qo_cast", s, h)] = cv
        cv = stream_unit(kv_pack[h], kvbuf[0:64, par * 512:(par + 1) * 512],
                         ["kv%d" % par], nrow=64)
        wp[("kv_cast", s, h)] = cv

    def head_setup(s, h):
        par = h % 2
        i = 0
        for kfc in range(2):
            for tt in range(TTN):
                bk = bank[5 + (i % 2)]
                waits = [("PL", wp[("kv_cast", s, h)]),
                         ("DVE", wp[("lat_evac", s, TTN - 1)]),
                         ("DVE", wp.get(("kT_evac", s, h, i - 2), 0))]
                if i == 0:
                    waits.append(("DVE", wp.get(("wo_evac", s, h * TTN - 1, 7), 0)))
                u = mm(bk[:, :], kvbuf[0:64, par * 512 + kfc * 128: par * 512 + (kfc + 1) * 128],
                       latT[:, tt * 512:(tt + 1) * 512], True, True, waits=waits)
                use_region("kv%d" % par)
                ev = v_op(lambda e, S_, b=bk, kfc_=kfc, tt_=tt: e.tensor_copy(
                    kT_sb[:, kfc_ * 2048 + tt_ * 512: kfc_ * 2048 + (tt_ + 1) * 512], b[:, :]),
                    waits=[("PE", u)], inc="DVE")
                wp[("kT_evac", s, h, i)] = ev
                i += 1
        for tc in range(16):
            bk = bank[5 + (tc % 2)]
            waits = [("DVE", wp.get(("V_evac", s, h, tc - 2), wp[("kT_evac", s, h, 7)]))]
            u = mm(bk[:, 0:256], latT[:, tc * 128:(tc + 1) * 128],
                   kvbuf[0:64, par * 512 + 256: par * 512 + 512], True, True, waits=waits)
            use_region("kv%d" % par)
            ev = v_op(lambda e, S_, b=bk, tc_=tc: e.tensor_copy(
                V_sb[:, tc_ * 256:(tc_ + 1) * 256], b[:, 0:256]),
                waits=[("PE", u)], inc="DVE")
            wp[("V_evac", s, h, tc)] = ev
        for qtr in range(4):
            for dfc in range(2):
                av = a_op(lambda e, S_, dfc_=dfc, q_=qtr: e.activation(
                    ksq_sb, kT_sb[:, dfc_ * 2048 + q_ * 512: dfc_ * 2048 + (q_ + 1) * 512],
                    AF.Square),
                    waits=[("DVE", wp[("kT_evac", s, h, 7)]),
                           ("PE", wp.get(("ksq_used", s, h, qtr, dfc - 1),
                                         wp.get(("ksq_used", s, h, qtr - 1, 1), 0)))],
                    inc="ACT")
                mm(bank[4][0:1, :], ones_f[:], ksq_sb, dfc == 0, dfc == 1,
                   waits=[("ACT", av), ("DVE", wp.get(("wo_evac", s, h * TTN - 1, 7), 0)),
                          ("DVE", wp.get(("ksq_copy", s, h, qtr - 1), 0))],
                   force_inc=True)
                wp[("ksq_used", s, h, qtr, dfc)] = P.now("PE")
            kcv = v_op(lambda e, S_, q_=qtr: e.tensor_copy(
                r_ksq[0:1, q_ * 512:(q_ + 1) * 512], bank[4][0:1, :]),
                waits=[("PE", wp[("ksq_used", s, h, qtr, 1)])], inc="DVE")
            wp[("ksq_copy", s, h, qtr)] = kcv
        v_drain()
        nk = v_op(lambda e, S_: e.tensor_reduce(r_negk, r_ksq, axis=AX.X, op=ALU.max, negate=True),
                  inc="DVE")
        v_drain()
        wp[("negk", s, h)] = nk

    def attn_unit(s, h, qt):
        par = h % 2
        ug = h * TTN + qt
        bias_v = bias_sb[:, h % 2: h % 2 + 1]
        for qfc in range(2):
            bk = bank[5 + qfc]
            for c in range(C):
                waits = [("PL", wp[("qo_cast", s, h)])]
                if c == 0:
                    waits.append(("DVE", wp.get(("qT_evac", s, ug - 1, qfc), 0)))
                    if qt == 0:
                        waits.append(("DVE", wp[("V_evac", s, h, 15)]))
                    if ug == 0:
                        waits.append(("DVE", wp.get(("seq_prev_dve", s), 0)))
                mm(bk[:, :], qo_bf[par][:, c * 256 + qfc * 128: c * 256 + (qfc + 1) * 128],
                   xlnT[:, c * 2048 + qt * 512: c * 2048 + (qt + 1) * 512],
                   c == 0, c == C - 1, waits=waits)
            use_region(REG_QO[par])
            u = P.now("PE")
            ev = v_op(lambda e, S_, b=bk, q_=qfc: e.tensor_copy(
                qT_sb[:, q_ * 512:(q_ + 1) * 512], b[:, :]),
                waits=[("PE", u)], inc="DVE")
            wp[("qT_evac", s, ug, qfc)] = ev
        if qt == 0:
            for qfc in range(2):
                av = a_op(lambda e, S_, q_=qfc: e.activation(
                    qsq_sb, qT_sb[:, q_ * 512:(q_ + 1) * 512], AF.Square),
                    waits=[("DVE", wp[("qT_evac", s, ug, qfc)]),
                           ("PE", wp.get(("nsq_used", s, h - 1, qfc - 1),
                                         wp.get(("nsq_used", s, h - 1, 1), 0)))],
                    inc="ACT")
                mm(bank[4][0:1, :], ones_f[:], qsq_sb, qfc == 0, qfc == 1,
                   waits=[("ACT", av), ("DVE", wp.get(("wo_evac", s, ug - 1, 7), 0))],
                   force_inc=True)
                wp[("nsq_used", s, h, qfc)] = P.now("PE")
            nq = v_op(lambda e, S_: e.tensor_reduce(r_negq, bank[4][0:1, :], axis=AX.X,
                                                    op=ALU.max, negate=True),
                      waits=[("PE", wp[("nsq_used", s, h, 1)]), ("DVE", wp[("negk", s, h)])],
                      inc="DVE")
            v_drain()
            ns = v_op(lambda e, S_: e.tensor_tensor(r_negs, r_negq, r_negk, op=ALU.add),
                      inc="DVE")
            v_drain()
            bm = mm(bank[4][:, 0:1], crow_1_32[:], r_negs, True, True, waits=[("DVE", ns)])
            bc = v_op(lambda e, S_, bv=bias_v: e.tensor_copy(bv, bank[4][:, 0:1]),
                      waits=[("PE", bm), ("ACT", wp.get(("bias_free_h", s, h - 2), 0))],
                      inc="DVE")
            wp[("bias_ready_h", s, h)] = bc
        wp[("bias_ready", s, ug)] = wp[("bias_ready_h", s, h)]

        def scores(kc):
            for dfc in range(2):
                waits = []
                if dfc == 0:
                    waits.append(("ACT", wp.get(("exp", s, ug, kc - 2), 0)))
                    if kc == 0:
                        waits.append(("DVE", wp[("qT_evac", s, ug, 1)]))
                        waits.append(("ACT", wp.get(("exp", s, ug - 1, 15),
                                                    wp.get(("seq_prev_act", s), 0))))
                mm(bank[kc % 2][:, :],
                   kT_sb[:, dfc * 2048 + kc * 128: dfc * 2048 + (kc + 1) * 128],
                   qT_sb[:, dfc * 512:(dfc + 1) * 512], dfc == 0, dfc == 1, waits=waits)
            wp[("scores", s, ug, kc)] = P.now("PE")

        def exp_chunk(kc):
            waits = [("PE", wp[("scores", s, ug, kc)]),
                     ("DVE", wp[("bias_ready", s, ug)]),
                     ("PE", wp.get(("oT_mm", s, ug, kc - 4), 0))]
            av = a_op(lambda e, S_, kc_=kc: e.activation(
                attnT[kc_ % 4], bank[kc_ % 2][:, :], AF.Exp, bias=bias_v, scale=1.0 / 16.0),
                waits=waits, inc="ACT")
            wp[("exp", s, ug, kc)] = av

        def sums_oT(kc):
            ew = wp[("exp", s, ug, kc)]
            w1 = [("ACT", ew)]
            if kc == 0:
                w1.append(("DVE", wp.get(("wo_evac", s, ug - 1, 7), 0)))
            mm(bank[4][0:1, :], ones_b[:], attnT[kc % 4], kc == 0, kc == 15,
               waits=w1)
            if kc == 15:
                wp[("sums_done", s, ug)] = P.now("PE")
            for vfc in range(2):
                w2 = []
                if kc == 0:
                    w2 = [("DVE", wp[("V_evac", s, h, 15)]),
                          ("DVE", wp.get(("oT_evac", s, ug - 1, vfc), 0))]
                mm(bank[2 + vfc][:, :], V_sb[:, kc * 256 + vfc * 128: kc * 256 + (vfc + 1) * 128],
                   attnT[kc % 4], kc == 0, kc == 15, waits=w2, force_inc=(vfc == 1))
            wp[("oT_mm", s, ug, kc)] = P.now("PE")

        for kc in range(16):
            scores(kc)
            exp_chunk(kc)
            if kc >= 1:
                sums_oT(kc - 1)
        sums_oT(15)
        if qt == TTN - 1:
            wp[("bias_free_h", s, h)] = wp[("exp", s, ug, 15)]

        v_op(lambda e, S_: e.tensor_copy(rrow_sb[:], bank[4][0:1, :]),
             waits=[("PE", wp[("sums_done", s, ug)])], inc="DVE")
        v_drain()
        rv = v_op(lambda e, S_: e.reciprocal(rrow_sb[:], rrow_sb[:]), inc="DVE")
        rbm = mm(bank[4][:, :], crow_1[:], rrow_sb[:], True, True, waits=[("DVE", rv)])
        rbc = v_op(lambda e, S_: e.tensor_copy(rb_sb, bank[4][:, :]),
                   waits=[("PE", rbm)], inc="DVE")
        wp[("rbc", s, ug)] = rbc
        for vfc in range(2):
            ev = v_op(lambda e, S_, v_=vfc: e.tensor_tensor(
                oT_sb[:, v_ * 512:(v_ + 1) * 512], bank[2 + v_][:, :], rb_sb, op=ALU.mult),
                waits=[("PE", wp[("oT_mm", s, ug, 15)]), ("DVE", rbc)], inc="DVE")
            wp[("oT_evac", s, ug, vfc)] = ev
        for dc in range(C):
            bk = bank[7 if dc % 2 == 0 else 4]
            for vfc in range(2):
                waits = []
                if vfc == 0:
                    waits.append(("DVE", wp[("oT_evac", s, ug, 1)]))
                    waits.append(("DVE", wp.get(("wo_evac", s, ug, dc - 2), 0)))
                    if dc == 1:
                        waits.append(("DVE", rbc))
                u = mm(bk[:, :], qo_bf[par][:, 2048 + vfc * 1024 + dc * 128:
                                            2048 + vfc * 1024 + (dc + 1) * 128],
                       oT_sb[:, vfc * 512:(vfc + 1) * 512], vfc == 0, vfc == 1, waits=waits)
            use_region(REG_QO[par])
            dst = x_run[:, dc * 2048 + qt * 512: dc * 2048 + (qt + 1) * 512]
            if h == 0:
                _w0 = [("PE", u), ("ST", wp.get(("stores_done", s), 0))]
                if stage in ("snap", "snapxg"):
                    _w0.append(("SN", P.now("SN")))
                ev = v_op(lambda e, S_, b=bk, d=dst: e.tensor_copy(d, b[:, :]),
                          waits=_w0, inc="DVE")
            else:
                ev = v_op(lambda e, S_, b=bk, d=dst: e.tensor_tensor(d, d, b[:, :], op=ALU.add),
                          waits=[("PE", u)], inc="DVE")
            wp[("wo_evac", s, ug, dc)] = ev

    def residual(s):
        for tt in range(TTN):
            w = [("PE", wp[("attn_done_pe", s)]),
                 ("DVE", wp[("wo_evac", s, H * TTN - 1, C - 1)])]
            if tt > 0:
                w.append(("DVE", wp[("res_consumed", s, tt - 1)]))
            ldv = dma(x_ring, x_img[s, tt], "XL", waits=w)
            for c in range(C):
                dst = x_run[:, c * 2048 + tt * 512: c * 2048 + (tt + 1) * 512]
                v_op(lambda e, S_, d=dst, c_=c: e.tensor_tensor(
                    d, d, x_ring[:, c_ * 512:(c_ + 1) * 512], op=ALU.add),
                    waits=[("XL", ldv)], inc="DVE" if c == C - 1 else None)
            wp[("res_consumed", s, tt)] = P.now("DVE")
        wp[("xmid_done", s)] = P.now("DVE")

    def stream_ffn_winsw(s, e_idx, g):
        par = (e_idx * NG + g) % 2
        cv = None
        for u in range(2):
            cv = stream_unit(ffn_pack[e_idx, g, u],
                             win_r[par][:, u * 2048:(u + 1) * 2048], [REG_WIN[par]])
        for u in range(2):
            cv = stream_unit(ffn_pack[e_idx, g, 2 + u],
                             wsw_r[par][:, u * 2048:(u + 1) * 2048], [REG_WSW[par]])
        wp[("ffn_w", s, e_idx, g)] = cv

    def stream_ffn_wout(s, e_idx, g):
        cv = None
        for u in range(2):
            cv = stream_unit(ffn_pack[e_idx, g, 4 + u],
                             wout_r[:, u * 2048:(u + 1) * 2048], ["wout"])
        wp[("ffn_wout", s, e_idx, g)] = cv

    def ffn_expert_groups(s):
        experts = list(range(NE)) if ffn_experts is None else list(ffn_experts)
        if not experts:
            return
        h_hist = []
        gb_state = {"last": 0}
        units = [(e, g) for e in experts for g in range(NG)]
        tiles = [(e, g, tt) for e in experts for g in range(NG) for tt in range(TTN)]

        def emit_gb(j):
            e_idx, g, tt = tiles[j]
            if e_idx == 0:
                return None
            gbm = mm(bank[3][:, :],
                     sel_sb[0:8, (e_idx - 1) * 128: e_idx * 128],
                     gatesT[0:8, tt * 512:(tt + 1) * 512], True, True,
                     waits=[("DVE", wp[("router_done", s, TTN - 1)]),
                            ("DVE", sel_ld), ("DVE", gb_state["last"]),
                            ("DVE", h_hist[-1] if h_hist else 0)])
            gbc = v_op(lambda e, S_: e.tensor_copy(gb_sb, bank[3][:, :]),
                       waits=[("PE", gbm),
                              ("PE", wp.get(("attnT_region_pe", s), 0))], inc="DVE")
            gb_state["gbc_for_sw"] = gbc
            gb_state["last"] = gbc
            v_drain()
            return gbc

        def emit_xg_all(j, gbc, xg_pe):
            e_idx, g, tt = tiles[j]
            src3 = xlnT[:].rearrange("p (c q) -> p c q", q=2048)[:, :, tt * 512:(tt + 1) * 512]
            dst3 = xg_sb.rearrange("p (c q) -> p c q", c=C)
            gb3 = gb_sb[:, None, :].broadcast_to([128, C, 512])
            v_op(lambda e, S_, s3=src3, d3=dst3, g3=gb3: e.tensor_tensor(
                d3, s3, g3, op=ALU.mult),
                waits=[("DVE", gbc), ("PE", xg_pe)], inc="DVE")
            return P.now("DVE")

        stream_ffn_winsw(s, *units[0])
        stream_ffn_wout(s, *units[0])
        xg_next = None
        for j, (e_idx, g, tt) in enumerate(tiles):
            ui = j // TTN
            par = (e_idx * NG + g) % 2
            if tt == 0 and ui + 1 < len(units):
                stream_ffn_winsw(s, *units[ui + 1])
            wv = wp[("ffn_w", s, e_idx, g)]
            wov = wp[("ffn_wout", s, e_idx, g)]
            if j == 0:
                gbc0 = emit_gb(0)
                if gbc0 is None:
                    xg_next = wp[("router_done", s, TTN - 1)]
                else:
                    xg_next = emit_xg_all(0, gbc0, wp[("slot_lo_pe", s)])
            xg_done = xg_next
            if e_idx > 0:
                rhs_c = [xg_sb[:, c * 512:(c + 1) * 512] for c in range(C)]
            else:
                rhs_c = [xlnT[:, c * 2048 + tt * 512: c * 2048 + (tt + 1) * 512]
                         for c in range(C)]

            for fc in range(4):
                mb = bank[fc % 2]
                sb_ = bank[2 + fc % 2]
                mid_waits0 = [("PL", wv), ("DVE", xg_done)]
                if len(h_hist) >= 2:
                    mid_waits0.append(("DVE", h_hist[-2]))
                for c in range(C):
                    mm(mb[:, :],
                       win_r[par][:, c * 512 + fc * 128: c * 512 + (fc + 1) * 128],
                       rhs_c[c], c == 0, c == C - 1,
                       waits=mid_waits0 if c == 0 else ())
                use_region(REG_WIN[par])
                mid_pe = P.now("PE")
                for c in range(C):
                    sww = []
                    if c == 0:
                        if len(h_hist) >= 2:
                            sww.append(("DVE", h_hist[-2]))
                        if fc % 2 == 1:
                            sww.append(("DVE", gb_state.get("gbc_for_sw", 0)))
                    mm(sb_[:, :],
                       wsw_r[par][:, c * 512 + fc * 128: c * 512 + (fc + 1) * 128],
                       rhs_c[c], c == 0, c == C - 1, waits=sww)
                use_region(REG_WSW[par])
                sw_pe = P.now("PE")
                aw = [("PE", mid_pe)]
                if len(h_hist) >= 1:
                    aw.append(("DVE", h_hist[-1]))
                av = a_op(lambda e, S_, m=mb, f_=fc: e.activation(
                    fftmp[f_ % 2], m[:, :], AF.Silu), waits=aw, inc="ACT")
                hw = [("ACT", av), ("PE", sw_pe)]
                if j == 0 and fc == 0:
                    hw.append(("PE", wp[("slot_lo_pe", s)]))
                if fc == 0:
                    hw.append(("PE", wp.get(("h_read", s), 0)))
                hv = v_op(lambda e, S_, s2=sb_, f_=fc: e.tensor_tensor(
                    h_sb[:, f_ * 512:(f_ + 1) * 512], fftmp[f_ % 2], s2[:, :],
                    op=ALU.mult), waits=hw, inc="DVE")
                h_hist.append(hv)

            xg_pe_now = P.now("PE")
            nxt_gbc = None
            if j + 1 < len(tiles):
                nxt_gbc = emit_gb(j + 1)
                if nxt_gbc is None:
                    xg_next = wp[("router_done", s, TTN - 1)]
                else:
                    xg_next = emit_xg_all(j + 1, nxt_gbc, xg_pe_now)

            for dc in range(C):
                ob = bank[4 + dc % 2]
                for fc in range(4):
                    waits = []
                    if fc == 0:
                        waits = [("PL", wov), ("DVE", h_hist[-1]),
                                 ("DVE", wp.get(("out_evac", s, e_idx, g, tt, dc - 2), 0))]
                        if j == 0 and dc == 0:
                            waits.append(("DVE", wp[("xmid_done", s)]))
                        if dc == 0:
                            waits.append(("DVE", wp.get(("out_prev", s), 0)))
                    mm(ob[:, :],
                       wout_r[:, fc * 1024 + dc * 128: fc * 1024 + (dc + 1) * 128],
                       h_sb[:, fc * 512:(fc + 1) * 512], fc == 0, fc == 3, waits=waits)
                use_region("wout")
                u = P.now("PE")
                dst = x_run[:, dc * 2048 + tt * 512: dc * 2048 + (tt + 1) * 512]
                sw_ = [("PE", u)]
                if stage == "snap":
                    sw_.append(("SN", P.now("SN")))
                ev = v_op(lambda e, S_, b=ob, d=dst: e.tensor_tensor(
                    d, d, b[:, :], op=ALU.add), waits=sw_, inc="DVE")
                wp[("out_evac", s, e_idx, g, tt, dc)] = ev

            wp[("out_prev", s)] = wp[("out_evac", s, e_idx, g, tt, C - 1)]
            if s == 0:
                snap_tile0("u%dtt%d" % (ui, tt))
            wp[("h_read", s)] = P.now("PE")
            if tt == TTN - 1 and ui + 1 < len(units):
                stream_ffn_wout(s, *units[ui + 1])

    def store_tile(s, tt, wait_dve):
        src = x_run[:].rearrange("p (c q) -> p c q", c=C)[:, :, tt * 512:(tt + 1) * 512]
        dst = out_img[s, tt].rearrange("p (c q) -> p c q", c=C)
        return dma(dst, src, "ST", waits=[("DVE", wait_dve)])

    for s in range(S):
        ln_tiles(s, "hbm")
        if stage == "ln1":
            for tt in range(TTN):
                for c in range(C):
                    v_op(lambda e, S_, c_=c, tt_=tt: e.tensor_copy(
                        x_run[:, c_ * 2048 + tt_ * 512: c_ * 2048 + (tt_ + 1) * 512],
                        xlnT[:, c_ * 2048 + tt_ * 512: c_ * 2048 + (tt_ + 1) * 512]),
                        waits=[("DVE", wp[("ln_apply", s, "hbm", tt)])],
                        inc="DVE" if c == C - 1 else None)
                store_tile(s, tt, P.now("DVE"))
            wp[("xring_free_dve", s + 1)] = P.now("DVE")
            wp[("xring_free_pe", s + 1)] = P.now("PE")
            wp[("seq_prev_dve", s + 1)] = P.now("DVE")
            wp[("seq_prev_act", s + 1)] = P.now("ACT")
            wp[("stores_done", s + 1)] = P.now("ST")
            continue
        stream_head(s, 0)
        for h in range(H):
            if h + 1 < H:
                stream_head(s, h + 1)
            head_setup(s, h)
            for qt in range(TTN):
                attn_unit(s, h, qt)
        wp[("attn_done_pe", s)] = P.now("PE")
        residual(s)
        if stage == "attn":
            for tt in range(TTN):
                store_tile(s, tt, wp[("xmid_done", s)])
            wp[("xring_free_dve", s + 1)] = P.now("DVE")
            wp[("xring_free_pe", s + 1)] = P.now("PE")
            wp[("seq_prev_dve", s + 1)] = P.now("DVE")
            wp[("seq_prev_act", s + 1)] = P.now("ACT")
            wp[("stores_done", s + 1)] = P.now("ST")
            continue
        ln_tiles(s, "xrun")
        if stage == "gates":
            gd = v_op(lambda e, S_: e.tensor_copy(x_run[0:8, 0:2048], gatesT[0:8, :]),
                      waits=[("DVE", wp[("router_done", s, TTN - 1)])], inc="DVE")
            P.emit("sync", lambda e, S_, s_=s: e.dma_start(
                out_img[s_, 0][0:8, 0:2048], x_run[0:8, 0:2048]),
                waits=[("DVE", gd)], inc="ST", inc_n=16)
            wp[("xring_free_dve", s + 1)] = P.now("DVE")
            wp[("xring_free_pe", s + 1)] = P.now("PE")
            wp[("seq_prev_dve", s + 1)] = P.now("DVE")
            wp[("seq_prev_act", s + 1)] = P.now("ACT")
            wp[("stores_done", s + 1)] = P.now("ST")
            continue
        wp[("slot_lo_pe", s)] = P.now("PE")
        wp[("attnT_region_pe", s)] = P.now("PE")
        ffn_expert_groups(s)
        _exps = list(range(NE)) if ffn_experts is None else list(ffn_experts)
        if stage != "xg" and not stage.startswith("hdump") and not stage.startswith("odump"):
            for tt in range(TTN):
                if _exps:
                    store_tile(s, tt, wp[("out_evac", s, _exps[-1], NG - 1, tt, C - 1)])
                else:
                    store_tile(s, tt, wp[("xmid_done", s)])
        wp[("xring_free_dve", s + 1)] = P.now("DVE")
        wp[("xring_free_pe", s + 1)] = P.now("PE")
        wp[("seq_prev_dve", s + 1)] = P.now("DVE")
        wp[("seq_prev_act", s + 1)] = P.now("ACT")
        wp[("stores_done", s + 1)] = P.now("ST")

    final_st = P.now("ST")
    final_sn = P.now("SN")
    def _fin(e, S_):
        if final_sn:
            e.wait_ge(S_["SN"], final_sn)
        return e.wait_ge(S_["ST"], final_st)
    P.emit("sync", _fin)

    es = ExitStack()
    sems = {}
    for name in ("WL", "WL0", "WL1", "XL", "ST", "PE", "DVE", "ACT", "PL", "SN"):
        sems[name] = es.enter_context(nc.semaphore(name))
    blk = es.enter_context(nc.Block())

    def mk(eng_name):
        def body(eng):
            for fn in P.q[eng_name]:
                fn(eng, sems)
        return body
    blk.tensor(mk("tensor"))
    blk.vector(mk("vector"))
    blk.scalar(mk("scalar"))
    blk.gpsimd(mk("gpsimd"))
    blk.sync(mk("sync"))
    es.close()
    return nc


def _np32(a):
    a = np.asarray(a)
    return a.astype(np.float32) if a.dtype != np.float32 else a


def _pack_weights(Wq, Wkv_down, Wk_up, Wv_up, Wo, router, w_in_sh, w_out_sh, w_sw_sh,
                  w_in_e, w_sw_e, w_out_e):
    f32 = np.float32
    Wq4 = _np32(Wq).reshape(H, HD, C, 128)
    qo0 = np.transpose(Wq4, (0, 3, 2, 1)).reshape(H, 128, 2048)
    Wo4 = _np32(Wo).reshape(D, H, 2, 128)
    qo1 = np.transpose(Wo4, (1, 3, 2, 0)).reshape(H, 128, 2048)
    qo_pack = np.ascontiguousarray(np.stack([qo0, qo1], axis=1))
    kvk = _np32(Wk_up).reshape(H, HD, LAT).transpose(0, 2, 1)
    kvv = _np32(Wv_up).reshape(H, HD, LAT).transpose(0, 2, 1)
    kv_pack = np.ascontiguousarray(np.concatenate([kvk, kvv], axis=2))
    wk4 = _np32(Wkv_down).reshape(LAT, C, 128).transpose(2, 1, 0).reshape(128, 512)
    r4 = _np32(router).reshape(C, 128, 8).transpose(1, 0, 2).reshape(128, 64)
    wr_pack = np.ascontiguousarray(np.concatenate([wk4, r4], axis=1))
    Win_all = np.concatenate([_np32(w_in_sh).T[None], _np32(w_in_e)], axis=0)
    Wsw_all = np.concatenate([_np32(w_sw_sh).T[None], _np32(w_sw_e)], axis=0)
    Wout_all = np.concatenate([_np32(w_out_sh).T[None], _np32(w_out_e)], axis=0)
    ffn = np.empty((NE, NG, 6, 128, 2048), f32)
    for e in range(NE):
        wi = Win_all[e].reshape(C, 128, NG, 512)
        ws = Wsw_all[e].reshape(C, 128, NG, 512)
        for g in range(NG):
            for u in range(2):
                ffn[e, g, u] = np.transpose(wi[4 * u:4 * u + 4, :, g, :], (1, 0, 2)).reshape(128, 2048)
                ffn[e, g, 2 + u] = np.transpose(ws[4 * u:4 * u + 4, :, g, :], (1, 0, 2)).reshape(128, 2048)
            wo = Wout_all[e][g * 512:(g + 1) * 512].reshape(4, 128, 1024)
            for u in range(2):
                ffn[e, g, 4 + u] = np.transpose(wo[2 * u:2 * u + 2], (1, 0, 2)).reshape(128, 2048)
    return qo_pack, kv_pack, wr_pack, np.ascontiguousarray(ffn)


def _pack_x(x_core):
    arr = _np32(x_core).reshape(S, TTN, TW, C, 128)
    return np.ascontiguousarray(np.transpose(arr, (0, 1, 4, 3, 2)).reshape(S, TTN, 128, C * TW))


def _unpack_out(o):
    arr = np.asarray(o, np.float32).reshape(S, TTN, 128, C, TW)
    return np.transpose(arr, (0, 1, 4, 3, 2)).reshape(S, T, D)


_nc_cache = {}


def kernel(L0_attn_x, L0_attn_Wq, L0_attn_Wkv_down, L0_attn_Wk_up, L0_attn_Wv_up,
           L0_attn_Wo, L0_ffn_router_DE, L0_ffn_w_in_shared_FD, L0_ffn_w_out_shared_DF,
           L0_ffn_w_swiglu_FD, L0_ffn_w_in_eDF, L0_ffn_w_swiglu_eDF, L0_ffn_w_out_eFD,
           stage="full", ffn_experts=None):
    qo_pack, kv_pack, wr_pack, ffn_pack = _pack_weights(
        L0_attn_Wq, L0_attn_Wkv_down, L0_attn_Wk_up, L0_attn_Wv_up, L0_attn_Wo,
        L0_ffn_router_DE, L0_ffn_w_in_shared_FD, L0_ffn_w_out_shared_DF,
        L0_ffn_w_swiglu_FD, L0_ffn_w_in_eDF, L0_ffn_w_swiglu_eDF, L0_ffn_w_out_eFD)
    ident = np.eye(128, dtype=np.float32)
    sel8 = np.zeros((8, 1024), dtype=np.float32)
    for e in range(8):
        sel8[e, e * 128:(e + 1) * 128] = 1.0
    x = _np32(L0_attn_x)

    key = (stage, tuple(ffn_experts) if ffn_experts is not None else None)
    if key not in _nc_cache:
        _nc_cache[key] = build_program(stage, ffn_experts)
    nc = _nc_cache[key]

    in_maps = []
    for c in range(NCORES):
        in_maps.append({
            "x_img": _pack_x(x[S * c:S * (c + 1)]),
            "qo_pack": qo_pack, "kv_pack": kv_pack, "wr_pack": wr_pack,
            "ffn_pack": ffn_pack, "ident": ident, "sel8": sel8,
        })
    res = run_bass_kernel_spmd(nc, in_maps, core_ids=list(range(NCORES)))
    if stage in ("gates", "xg", "snap", "snapxg") or stage.startswith("hdump") or stage.startswith("odump"):
        return [np.asarray(r["out_img"]) for r in res.results]
    outs = [_unpack_out(r["out_img"]) for r in res.results]
    return np.concatenate(outs, axis=0)


# version 11
# speedup vs baseline: 1.0073x; 1.0073x over previous
import os
import numpy as np
from contextlib import ExitStack

import concourse.bass as bass
import concourse.mybir as mybir
from concourse.bass_utils import run_bass_kernel_spmd

F32 = mybir.dt.float32
BF16 = mybir.dt.bfloat16
U8 = mybir.dt.uint8
AF = mybir.ActivationFunctionType
ALU = mybir.AluOpType
AX = mybir.AxisListType

B, T, D, H, HD, LAT = 16, 2048, 1024, 8, 256, 64
NCORES = 8
S = 2
C = 8
TTN = 4
TW = 512
NE = 9
NG = 4
EPS = 1e-5
BIG = 1.0e5
KB = 1024


PARANOID = [False]
DRAIN_ALL = [os.environ.get("BASS_SIM_FP64") == "1"]


class Plan:
    ENGS = ("tensor", "vector", "scalar", "gpsimd", "sync")

    def __init__(self):
        self.q = {e: [] for e in self.ENGS}
        self.cnt = {}
        self.last_wait = {e: {} for e in self.ENGS}

    def sem(self, name):
        self.cnt[name] = 0

    def now(self, name):
        return self.cnt[name]

    def emit(self, eng, fn, waits=(), inc=None, inc_n=1):
        lw = self.last_wait[eng]
        eff = []
        if PARANOID[0]:
            waits = list(waits) + [(k, self.cnt[k]) for k in self.cnt]
        for s_, v in waits:
            if v <= 0 or lw.get(s_, -1) >= v:
                continue
            lw[s_] = v
            eff.append((s_, v))
        if inc is not None:
            self.cnt[inc] += inc_n

        da = DRAIN_ALL[0] and eng in ("vector", "scalar", "gpsimd")

        def run(e, sems, _eff=tuple(eff), _fn=fn, _inc=inc, _n=inc_n, _da=da):
            for s_, v in _eff:
                e.wait_ge(sems[s_], v)
            if _da:
                e.drain()
            inst = _fn(e, sems)
            if _inc is not None:
                inst.then_inc(sems[_inc], _n)
        self.q[eng].append(run)
        return self.cnt[inc] if inc is not None else None


def build_program(stage="full", ffn_experts=None):
    nc = bass.Bass("TRN2", target_bir_lowering=False, debug=False)

    x_img = nc.dram_tensor("x_img", [S, TTN, 128, C * TW], F32, kind="ExternalInput").ap()
    qo_pack = nc.dram_tensor("qo_pack", [H, 2, 128, 2048], F32, kind="ExternalInput").ap()
    kv_pack = nc.dram_tensor("kv_pack", [H, 64, 512], F32, kind="ExternalInput").ap()
    wr_pack = nc.dram_tensor("wr_pack", [128, 576], F32, kind="ExternalInput").ap()
    ffn_pack = nc.dram_tensor("ffn_pack", [NE, NG, 6, 128, 2048], F32, kind="ExternalInput").ap()
    ident_in = nc.dram_tensor("ident", [128, 128], F32, kind="ExternalInput").ap()
    sel8_in = nc.dram_tensor("sel8", [8, 1024], F32, kind="ExternalInput").ap()
    out_img = nc.dram_tensor("out_img", [S, TTN, 128, C * TW], F32, kind="ExternalOutput").ap()
    snaps = nc.dram_tensor("snaps", [16, 128, C * TW], F32, kind="ExternalOutput").ap() \
        if stage in ("snap", "snapxg") else None

    al = nc.alloc_sbuf_tensor
    x_run = al("x_run", [128, C * 2048], F32)
    xlnT = al("xlnT", [128, C * 2048], BF16)
    lg = al("lg", [128, 4 * KB], U8)
    latT = lg.bitcast(BF16)[0:64, 0:2048]
    gatesT = lg.bitcast(BF16)[0:8, 0:2048]

    slot = al("slot", [128, 30 * KB], U8)
    slot_b = slot.bitcast(BF16)
    slot_f = slot.bitcast(F32)
    kT_sb = slot_b[:, 0:4096]
    V_sb = slot_b[:, 4096:8192]
    qT_sb = slot_b[:, 8192:9216]
    attnT = [slot_b[:, 9216 + i * 512: 9216 + (i + 1) * 512] for i in range(4)]
    oT_sb = slot_b[:, 11264:12288]
    rb_sb = slot_f[:, 6144:6656]
    qsq_sb = slot_f[:, 6656:7168]
    ksq_sb = slot_f[:, 7168:7680]
    x_ring = slot_f[:, 0:4096]
    xln2f = slot_f[:, 0:4096]
    xg_sb = slot_b[:, 0:4096]
    h_sb = slot_b[:, 4096:6144]
    fftmp = [slot_f[:, 6144 + i * 512: 6144 + (i + 1) * 512] for i in range(2)]
    gb_sb = slot_b[:, 10240:10752]

    war = al("war", [128, 56 * KB], U8)
    war_f = war.bitcast(F32)
    war_b = war.bitcast(BF16)
    stage_f2 = [war_f[:, 0:2048], war_f[:, 2048:4096]]
    win_r = {0: war_b[:, 8192:12288], 1: war_b[:, 16384:20480]}
    wsw_r = {0: war_b[:, 12288:16384], 1: war_b[:, 20480:24576]}
    wout_r = war_b[:, 24576:28672]
    qo_bf = {0: war_b[:, 16384:20480], 1: war_b[:, 20480:24576]}
    REG_WIN = {0: "A_lo", 1: "B_lo"}
    REG_WSW = {0: "A_hi", 1: "B_hi"}
    REG_QO = {0: "B_lo", 1: "B_hi"}

    kvbuf = al("kvbuf", [64, 1024], BF16)
    wr_f = al("wr_f", [128, 576], F32)
    wkvd_bf = al("wkvd_bf", [128, 512], BF16)
    ident_f = al("ident_f", [128, 128], F32)
    sel_sb = al("sel_sb", [8, 1024], BF16)
    rows = al("rows", [1, 2564], F32)
    r_srow = rows[0:1, 0:512]
    r_ssrow = rows[0:1, 512:1024]
    r_mrow = rows[0:1, 1024:1536]
    r_m2row = rows[0:1, 1536:2048]
    r_ksq = rows[0:1, 0:2048]
    r_rstd = rows[0:1, 2048:2560]
    r_negq = rows[0:1, 2560:2561]
    r_negk = rows[0:1, 2561:2562]
    r_negs = rows[0:1, 2562:2563]
    ones_f = al("ones_f", [128, 1], F32)
    ones_b = al("ones_b", [128, 1], BF16)
    crow_1 = al("crow_1", [1, 128], F32)
    crow_1b = al("crow_1b", [1, 128], BF16)
    crow_n1024 = al("crow_n1024", [1, 128], F32)
    crow_1_32 = al("crow_1_32", [1, 128], F32)
    bias_sb = al("bias_sb", [128, 2], F32)
    sc_sb = al("sc_sb", [128, 8], F32)
    sg_sb = al("sg_sb", [128, 8], F32)
    ind_sb = al("ind_sb", [128, 8], F32)
    sx_sb = al("sx_sb", [128, 8], F32)
    gtm_sb = al("gtm_sb", [128, 8], F32)
    m1_sb = al("m1_sb", [128, 1], F32)
    m2_sb = al("m2_sb", [128, 1], F32)
    rrow_sb = rows[0:1, 2048:2560]

    bank = [nc.alloc_psum_tensor("bank%d" % i, [128, 512], F32) for i in range(8)]

    P = Plan()
    for s_ in ("WL", "WL0", "WL1", "XL", "ST", "PE", "DVE", "ACT", "PL", "SN"):
        P.sem(s_)
    snap_i = [0]

    def snap_buf(src_ap2, dst_rearr=True):
        if snap_i[0] >= 16:
            return
        i = snap_i[0]
        snap_i[0] += 1
        dst_ap = snaps[i][0:src_ap2.shape[0], 0:src_ap2.shape[1]]
        P.emit("sync", lambda e, S_, s2=src_ap2, d2=dst_ap: e.dma_start(d2, s2),
               waits=[("DVE", P.now("DVE"))], inc="SN", inc_n=16)

    def snap_tile0(tag):
        if stage != "snap" or snap_i[0] >= 16:
            return
        i = snap_i[0]
        snap_i[0] += 1
        src_ap = x_run[:].rearrange("p (c q) -> p c q", c=C)[:, :, 0:512]
        dst_ap = snaps[i].rearrange("p (c q) -> p c q", c=C)
        P.emit("sync", lambda e, S_, s2=src_ap, d2=dst_ap: e.dma_start(d2, s2),
               waits=[("DVE", P.now("DVE"))], inc="SN", inc_n=16)
    wp = {}
    region_pe = {}

    def use_region(name):
        region_pe[name] = P.now("PE")

    def dma(out_ap, in_ap, sem, waits=()):
        return P.emit("sync", lambda e, S_, o=out_ap, i=in_ap: e.dma_start(o, i),
                      waits=waits, inc=sem, inc_n=16)

    def mm(out_ap, lhsT, rhs, start, stop, waits=(), force_inc=False):
        return P.emit("tensor",
                      lambda e, S_, o=out_ap, l=lhsT, r=rhs, st=start, sp=stop:
                      e.matmul(o, l, r, start=st, stop=sp),
                      waits=waits, inc="PE" if (stop or force_inc) else None)

    def v_op(fn, waits=(), inc=None):
        return P.emit("vector", fn, waits=waits, inc=inc)

    def a_op(fn, waits=(), inc=None):
        return P.emit("scalar", fn, waits=waits, inc=inc)

    def v_drain():
        P.emit("vector", lambda e, S_: e.drain())

    st_state = {"i": 0, "cast": {0: 0, 1: 0}}

    def stream_unit(src_ap, dest_ap, regions, nrow=128):
        par = st_state["i"] % 2
        st_state["i"] += 1
        wsem = "WL%d" % par
        st = stage_f2[par] if nrow == 128 else stage_f2[par][0:nrow, 0:512]
        dv = dma(st, src_ap, wsem, waits=[("PL", st_state["cast"][par])])
        waits = [(wsem, dv)] + [("PE", region_pe.get(r, 0)) for r in regions]
        cv = P.emit("gpsimd", lambda e, S_, o=dest_ap, i=st: e.tensor_copy(o, i),
                    waits=waits, inc="PL")
        st_state["cast"][par] = cv
        return cv

    v_op(lambda e, S_: e.memset(ones_f[:], 1.0), inc="DVE")
    v_op(lambda e, S_: e.memset(ones_b[:], 1.0), inc="DVE")
    v_op(lambda e, S_: e.memset(crow_1[:], 1.0), inc="DVE")
    v_op(lambda e, S_: e.memset(crow_1b[:], 1.0), inc="DVE")
    v_op(lambda e, S_: e.memset(crow_n1024[:], -1.0 / 1024.0), inc="DVE")
    v_op(lambda e, S_: e.memset(crow_1_32[:], 1.0 / 32.0), inc="DVE")
    consts_done = P.now("DVE")
    dma(ident_f[:], ident_in, "WL")
    dma(wr_f[:], wr_pack, "WL")
    sel_f = stage_f2[0][0:8, 0:1024]
    preload_ld = dma(sel_f, sel8_in, "WL")
    ident_ld = wr_ld = sel_ld0 = preload_ld
    sel_ld = v_op(lambda e, S_: e.tensor_copy(sel_sb[:], sel_f),
                  waits=[("WL", sel_ld0)], inc="DVE")
    sel_pl = P.emit("gpsimd", lambda e, S_: e.engine_nop(), waits=[("DVE", sel_ld)], inc="PL")
    st_state["cast"][0] = sel_pl
    wkvd_done = v_op(lambda e, S_: e.tensor_copy(wkvd_bf[:], wr_f[:, 0:512]),
                     waits=[("WL", wr_ld)], inc="DVE")

    def ln_tiles(s, kind):
        for tt in range(TTN):
            if kind == "hbm":
                w = [("DVE", wp.get(("xring_free_dve", s), 0)),
                     ("PE", wp.get(("xring_free_pe", s), 0))]
                if tt > 0:
                    w.append(("DVE", wp[("xring_consumed", s, tt - 1)]))
                if stage in ("snap", "snapxg"):
                    w.append(("SN", P.now("SN")))
                ldv = dma(x_ring, x_img[s, tt], "XL", waits=w)
                xs = [x_ring[:, c * 512:(c + 1) * 512] for c in range(C)]
                base = [("XL", ldv), ("DVE", wp.get(("seq_prev_dve", s), 0))]
            else:
                xs = [x_run[:, c * 2048 + tt * 512: c * 2048 + (tt + 1) * 512] for c in range(C)]
                base = [("DVE", wp[("xmid_done", s)]), ("PE", wp[("attn_done_pe", s)])]

            for c in range(C):
                waits = list(base)
                if c == 0:
                    waits += [("DVE", consts_done),
                              ("DVE", wp.get(("ln_apply", s, kind, tt - 1), 0)),
                              ("DVE", wp.get(("lat_evac", s, tt - 1), 0) if kind == "hbm" else
                               wp.get(("router_done", s, tt - 1), 0))]
                mm(bank[4][0:1, :], ones_f[:], xs[c], c == 0, c == C - 1, waits=waits)
            sum_pe = P.now("PE")
            for c in range(C):
                sq = qsq_sb if c % 2 == 0 else ksq_sb
                aw = list(base) + [("PE", wp.get(("ln_sq_used", s, kind, tt, c - 2), 0))]
                av = a_op(lambda e, S_, o=sq, i=xs[c]: e.activation(o, i, AF.Square),
                          waits=aw, inc="ACT")
                mm(bank[5][0:1, :], ones_f[:], sq, c == 0, c == C - 1, waits=[("ACT", av)],
                   force_inc=True)
                wp[("ln_sq_used", s, kind, tt, c)] = P.now("PE")
            ssum_pe = P.now("PE")

            v_op(lambda e, S_: e.tensor_copy(r_srow, bank[4][0:1, :]),
                 waits=[("PE", sum_pe)], inc="DVE")
            v_op(lambda e, S_: e.tensor_copy(r_ssrow, bank[5][0:1, :]),
                 waits=[("PE", ssum_pe)], inc="DVE")
            v_drain()
            v_op(lambda e, S_: e.tensor_scalar(r_mrow, r_srow, 1.0 / 1024.0, None, op0=ALU.mult))
            v_drain()
            v_op(lambda e, S_: e.tensor_tensor(r_m2row, r_mrow, r_mrow, op=ALU.mult))
            v_op(lambda e, S_: e.tensor_scalar(r_ssrow, r_ssrow, 1.0 / 1024.0, None, op0=ALU.mult))
            v_drain()
            v_op(lambda e, S_: e.tensor_tensor(r_ssrow, r_ssrow, r_m2row, op=ALU.subtract))
            v_drain()
            var_done = v_op(lambda e, S_: e.tensor_scalar(r_ssrow, r_ssrow, EPS, None, op0=ALU.add),
                            inc="DVE")
            lnv = a_op(lambda e, S_: e.activation(r_rstd, r_ssrow, AF.Ln),
                       waits=[("DVE", var_done), ("PE", wp.get(("rs_mm", s), 0))], inc="ACT")
            P.emit("scalar", lambda e, S_: e.drain())
            rstd_done = a_op(lambda e, S_: e.activation(r_rstd, r_rstd, AF.Exp, scale=-0.5),
                             waits=[("ACT", lnv)], inc="ACT")
            srow_dve = P.now("DVE")
            mm(bank[6][:, :], crow_n1024[:], r_srow, True, True, waits=[("DVE", srow_dve)])
            rs = mm(bank[7][:, :], crow_1[:], r_rstd, True, True, waits=[("ACT", rstd_done)])
            wp[("rs_mm", s)] = rs
            def _emit_t(c):
                w0 = [("PE", rs)]
                if kind == "xrun" and c == 0:
                    w0.append(("PE", wp.get(("router_pe", s, tt - 1), 0)))
                v_op(lambda e, S_, x_=xs[c], f_=fftmp[c % 2]: e.tensor_tensor(
                    f_, x_, bank[6][:, :], op=ALU.add), waits=w0)

            def _emit_out(c):
                dst = xlnT[:, c * 2048 + tt * 512: c * 2048 + (tt + 1) * 512]
                v_op(lambda e, S_, d=dst, f_=fftmp[c % 2]: e.tensor_tensor(
                    d, f_, bank[7][:, :], op=ALU.mult), inc="DVE")
                if kind == "xrun":
                    v_op(lambda e, S_, c_=c, f_=fftmp[c % 2]: e.tensor_tensor(
                        xln2f[:, c_ * 512:(c_ + 1) * 512], f_, bank[7][:, :], op=ALU.mult),
                        inc="DVE")

            _emit_t(0)
            for c in range(1, C):
                _emit_t(c)
                _emit_out(c - 1)
            _emit_out(C - 1)
            wp[("ln_apply", s, kind, tt)] = P.now("DVE")
            if kind == "hbm":
                wp[("xring_consumed", s, tt)] = P.now("DVE")
                for c in range(C):
                    waits = [("DVE", wp[("ln_apply", s, "hbm", tt)])]
                    if c == 0:
                        waits += [("DVE", wkvd_done), ("DVE", wp.get(("lat_evac", s, tt - 1), 0))]
                    mm(bank[5][0:64, :], wkvd_bf[:, c * 64:(c + 1) * 64],
                       xlnT[:, c * 2048 + tt * 512: c * 2048 + (tt + 1) * 512],
                       c == 0, c == C - 1, waits=waits)
                u = P.now("PE")
                ev = v_op(lambda e, S_, tt_=tt: e.tensor_copy(
                    latT[:, tt_ * 512:(tt_ + 1) * 512], bank[5][0:64, :]),
                    waits=[("PE", u)], inc="DVE")
                wp[("lat_evac", s, tt)] = ev
            else:
                router_tile(s, tt)

    def router_tile(s, tt):
        for tc in range(4):
            for c in range(C):
                waits = [("DVE", wp[("ln_apply", s, "xrun", tt)])]
                if c == 0:
                    waits += [("WL", wr_ld), ("DVE", wp.get(("sc_evac", s, tt, tc - 1), 0))]
                mm(bank[4][:, 0:8], xln2f[:, c * 512 + tc * 128: c * 512 + (tc + 1) * 128],
                   wr_f[:, 512 + c * 8: 512 + (c + 1) * 8], c == 0, c == C - 1, waits=waits)
            u = P.now("PE")
            sc = v_op(lambda e, S_: e.tensor_copy(sc_sb[:], bank[4][:, 0:8]),
                      waits=[("PE", u)], inc="DVE")
            wp[("sc_evac", s, tt, tc)] = sc
            v_drain()
            v_op(lambda e, S_: e.tensor_reduce(m1_sb[:], sc_sb[:], axis=AX.X, op=ALU.max))
            v_drain()
            v_op(lambda e, S_: e.tensor_scalar(ind_sb[:], sc_sb[:], m1_sb[:], None, op0=ALU.is_ge))
            v_drain()
            v_op(lambda e, S_: e.scalar_tensor_tensor(sx_sb[:], ind_sb[:], -BIG, sc_sb[:],
                                                      op0=ALU.mult, op1=ALU.add))
            v_drain()
            v_op(lambda e, S_: e.tensor_reduce(m2_sb[:], sx_sb[:], axis=AX.X, op=ALU.max))
            v_drain()
            v_op(lambda e, S_: e.tensor_scalar(ind_sb[:], sc_sb[:], m2_sb[:], None, op0=ALU.is_ge),
                 inc="DVE")
            av = a_op(lambda e, S_: e.activation(sg_sb[:], sc_sb[:], AF.Exp, scale=-1.0),
                      waits=[("DVE", sc)], inc="ACT")
            v_op(lambda e, S_: e.tensor_scalar(sg_sb[:], sg_sb[:], 1.0, None, op0=ALU.add),
                 waits=[("ACT", av)])
            v_drain()
            v_op(lambda e, S_: e.reciprocal(sg_sb[:], sg_sb[:]))
            v_drain()
            gv = v_op(lambda e, S_: e.tensor_tensor(gtm_sb[:], sg_sb[:], ind_sb[:], op=ALU.mult),
                      inc="DVE")
            tr = P.emit("tensor",
                        lambda e, S_: e.transpose(bank[5][0:8, 0:128], gtm_sb[:], ident_f[:]),
                        waits=[("DVE", gv), ("WL", ident_ld),
                               ("DVE", wp.get(("gT_evac", s, tt, tc - 1), 0))], inc="PE")
            ge = v_op(lambda e, S_, tc_=tc: e.tensor_copy(
                gatesT[0:8, tt * 512 + tc_ * 128: tt * 512 + (tc_ + 1) * 128],
                bank[5][0:8, 0:128]),
                waits=[("PE", tr)], inc="DVE")
            wp[("gT_evac", s, tt, tc)] = ge
        wp[("router_pe", s, tt)] = P.now("PE")
        wp[("router_done", s, tt)] = P.now("DVE")

    def stream_head(s, h):
        par = h % 2
        cv = None
        for u in range(2):
            cv = stream_unit(qo_pack[h, u], qo_bf[par][:, u * 2048:(u + 1) * 2048],
                             [REG_QO[par]])
        wp[("qo_cast", s, h)] = cv
        cv = stream_unit(kv_pack[h], kvbuf[0:64, par * 512:(par + 1) * 512],
                         ["kv%d" % par], nrow=64)
        wp[("kv_cast", s, h)] = cv

    def head_setup(s, h):
        par = h % 2
        i = 0
        for kfc in range(2):
            for tt in range(TTN):
                bk = bank[5 + (i % 2)]
                waits = [("PL", wp[("kv_cast", s, h)]),
                         ("DVE", wp[("lat_evac", s, TTN - 1)]),
                         ("DVE", wp.get(("kT_evac", s, h, i - 2), 0))]
                if i == 0:
                    waits.append(("DVE", wp.get(("wo_evac", s, h * TTN - 1, 7), 0)))
                u = mm(bk[:, :], kvbuf[0:64, par * 512 + kfc * 128: par * 512 + (kfc + 1) * 128],
                       latT[:, tt * 512:(tt + 1) * 512], True, True, waits=waits)
                use_region("kv%d" % par)
                ev = v_op(lambda e, S_, b=bk, kfc_=kfc, tt_=tt: e.tensor_copy(
                    kT_sb[:, kfc_ * 2048 + tt_ * 512: kfc_ * 2048 + (tt_ + 1) * 512], b[:, :]),
                    waits=[("PE", u)], inc="DVE")
                wp[("kT_evac", s, h, i)] = ev
                i += 1
        for tc in range(16):
            bk = bank[5 + (tc % 2)]
            waits = [("DVE", wp.get(("V_evac", s, h, tc - 2), wp[("kT_evac", s, h, 7)]))]
            u = mm(bk[:, 0:256], latT[:, tc * 128:(tc + 1) * 128],
                   kvbuf[0:64, par * 512 + 256: par * 512 + 512], True, True, waits=waits)
            use_region("kv%d" % par)
            ev = v_op(lambda e, S_, b=bk, tc_=tc: e.tensor_copy(
                V_sb[:, tc_ * 256:(tc_ + 1) * 256], b[:, 0:256]),
                waits=[("PE", u)], inc="DVE")
            wp[("V_evac", s, h, tc)] = ev
        for qtr in range(4):
            for dfc in range(2):
                av = a_op(lambda e, S_, dfc_=dfc, q_=qtr: e.activation(
                    ksq_sb, kT_sb[:, dfc_ * 2048 + q_ * 512: dfc_ * 2048 + (q_ + 1) * 512],
                    AF.Square),
                    waits=[("DVE", wp[("kT_evac", s, h, 7)]),
                           ("PE", wp.get(("ksq_used", s, h, qtr, dfc - 1),
                                         wp.get(("ksq_used", s, h, qtr - 1, 1), 0)))],
                    inc="ACT")
                mm(bank[4][0:1, :], ones_f[:], ksq_sb, dfc == 0, dfc == 1,
                   waits=[("ACT", av), ("DVE", wp.get(("wo_evac", s, h * TTN - 1, 7), 0)),
                          ("DVE", wp.get(("ksq_copy", s, h, qtr - 1), 0))],
                   force_inc=True)
                wp[("ksq_used", s, h, qtr, dfc)] = P.now("PE")
            kcv = v_op(lambda e, S_, q_=qtr: e.tensor_copy(
                r_ksq[0:1, q_ * 512:(q_ + 1) * 512], bank[4][0:1, :]),
                waits=[("PE", wp[("ksq_used", s, h, qtr, 1)])], inc="DVE")
            wp[("ksq_copy", s, h, qtr)] = kcv
        v_drain()
        nk = v_op(lambda e, S_: e.tensor_reduce(r_negk, r_ksq, axis=AX.X, op=ALU.max, negate=True),
                  inc="DVE")
        v_drain()
        wp[("negk", s, h)] = nk

    def attn_unit(s, h, qt):
        par = h % 2
        ug = h * TTN + qt
        bias_v = bias_sb[:, h % 2: h % 2 + 1]
        for qfc in range(2):
            bk = bank[5 + qfc]
            for c in range(C):
                waits = [("PL", wp[("qo_cast", s, h)])]
                if c == 0:
                    waits.append(("DVE", wp.get(("qT_evac", s, ug - 1, qfc), 0)))
                    if qt == 0:
                        waits.append(("DVE", wp[("V_evac", s, h, 15)]))
                    if ug == 0:
                        waits.append(("DVE", wp.get(("seq_prev_dve", s), 0)))
                mm(bk[:, :], qo_bf[par][:, c * 256 + qfc * 128: c * 256 + (qfc + 1) * 128],
                   xlnT[:, c * 2048 + qt * 512: c * 2048 + (qt + 1) * 512],
                   c == 0, c == C - 1, waits=waits)
            use_region(REG_QO[par])
            u = P.now("PE")
            ev = v_op(lambda e, S_, b=bk, q_=qfc: e.tensor_copy(
                qT_sb[:, q_ * 512:(q_ + 1) * 512], b[:, :]),
                waits=[("PE", u)], inc="DVE")
            wp[("qT_evac", s, ug, qfc)] = ev
        if qt == 0:
            for qfc in range(2):
                av = a_op(lambda e, S_, q_=qfc: e.activation(
                    qsq_sb, qT_sb[:, q_ * 512:(q_ + 1) * 512], AF.Square),
                    waits=[("DVE", wp[("qT_evac", s, ug, qfc)]),
                           ("PE", wp.get(("nsq_used", s, h - 1, qfc - 1),
                                         wp.get(("nsq_used", s, h - 1, 1), 0)))],
                    inc="ACT")
                mm(bank[4][0:1, :], ones_f[:], qsq_sb, qfc == 0, qfc == 1,
                   waits=[("ACT", av), ("DVE", wp.get(("wo_evac", s, ug - 1, 7), 0))],
                   force_inc=True)
                wp[("nsq_used", s, h, qfc)] = P.now("PE")
            nq = v_op(lambda e, S_: e.tensor_reduce(r_negq, bank[4][0:1, :], axis=AX.X,
                                                    op=ALU.max, negate=True),
                      waits=[("PE", wp[("nsq_used", s, h, 1)]), ("DVE", wp[("negk", s, h)])],
                      inc="DVE")
            v_drain()
            ns = v_op(lambda e, S_: e.tensor_tensor(r_negs, r_negq, r_negk, op=ALU.add),
                      inc="DVE")
            v_drain()
            bm = mm(bank[4][:, 0:1], crow_1_32[:], r_negs, True, True, waits=[("DVE", ns)])
            bc = v_op(lambda e, S_, bv=bias_v: e.tensor_copy(bv, bank[4][:, 0:1]),
                      waits=[("PE", bm), ("ACT", wp.get(("bias_free_h", s, h - 2), 0))],
                      inc="DVE")
            wp[("bias_ready_h", s, h)] = bc
        wp[("bias_ready", s, ug)] = wp[("bias_ready_h", s, h)]

        def scores(kc):
            for dfc in range(2):
                waits = []
                if dfc == 0:
                    waits.append(("ACT", wp.get(("exp", s, ug, kc - 2), 0)))
                    if kc == 0:
                        waits.append(("DVE", wp[("qT_evac", s, ug, 1)]))
                        waits.append(("ACT", wp.get(("exp", s, ug - 1, 15),
                                                    wp.get(("seq_prev_act", s), 0))))
                mm(bank[kc % 2][:, :],
                   kT_sb[:, dfc * 2048 + kc * 128: dfc * 2048 + (kc + 1) * 128],
                   qT_sb[:, dfc * 512:(dfc + 1) * 512], dfc == 0, dfc == 1, waits=waits)
            wp[("scores", s, ug, kc)] = P.now("PE")

        def exp_chunk(kc):
            waits = [("PE", wp[("scores", s, ug, kc)]),
                     ("DVE", wp[("bias_ready", s, ug)]),
                     ("PE", wp.get(("oT_mm", s, ug, kc - 4), 0))]
            av = a_op(lambda e, S_, kc_=kc: e.activation(
                attnT[kc_ % 4], bank[kc_ % 2][:, :], AF.Exp, bias=bias_v, scale=1.0 / 16.0),
                waits=waits, inc="ACT")
            wp[("exp", s, ug, kc)] = av

        def sums_oT(kc):
            ew = wp[("exp", s, ug, kc)]
            w1 = [("ACT", ew)]
            if kc == 0:
                w1.append(("DVE", wp.get(("wo_evac", s, ug - 1, 7), 0)))
            mm(bank[4][0:1, :], ones_b[:], attnT[kc % 4], kc == 0, kc == 15,
               waits=w1)
            if kc == 15:
                wp[("sums_done", s, ug)] = P.now("PE")
            for vfc in range(2):
                w2 = []
                if kc == 0:
                    w2 = [("DVE", wp[("V_evac", s, h, 15)]),
                          ("DVE", wp.get(("oT_evac", s, ug - 1, vfc), 0))]
                mm(bank[2 + vfc][:, :], V_sb[:, kc * 256 + vfc * 128: kc * 256 + (vfc + 1) * 128],
                   attnT[kc % 4], kc == 0, kc == 15, waits=w2, force_inc=(vfc == 1))
            wp[("oT_mm", s, ug, kc)] = P.now("PE")

        for kc in range(16):
            scores(kc)
            exp_chunk(kc)
            if kc >= 1:
                sums_oT(kc - 1)
        sums_oT(15)
        if qt == TTN - 1:
            wp[("bias_free_h", s, h)] = wp[("exp", s, ug, 15)]

        v_op(lambda e, S_: e.tensor_copy(rrow_sb[:], bank[4][0:1, :]),
             waits=[("PE", wp[("sums_done", s, ug)])], inc="DVE")
        v_drain()
        rv = v_op(lambda e, S_: e.reciprocal(rrow_sb[:], rrow_sb[:]), inc="DVE")
        rbm = mm(bank[4][:, :], crow_1[:], rrow_sb[:], True, True, waits=[("DVE", rv)])
        rbc = v_op(lambda e, S_: e.tensor_copy(rb_sb, bank[4][:, :]),
                   waits=[("PE", rbm)], inc="DVE")
        wp[("rbc", s, ug)] = rbc
        for vfc in range(2):
            ev = v_op(lambda e, S_, v_=vfc: e.tensor_tensor(
                oT_sb[:, v_ * 512:(v_ + 1) * 512], bank[2 + v_][:, :], rb_sb, op=ALU.mult),
                waits=[("PE", wp[("oT_mm", s, ug, 15)]), ("DVE", rbc)], inc="DVE")
            wp[("oT_evac", s, ug, vfc)] = ev
        for dc in range(C):
            bk = bank[7 if dc % 2 == 0 else 4]
            for vfc in range(2):
                waits = []
                if vfc == 0:
                    waits.append(("DVE", wp[("oT_evac", s, ug, 1)]))
                    waits.append(("DVE", wp.get(("wo_evac", s, ug, dc - 2), 0)))
                    if dc == 1:
                        waits.append(("DVE", rbc))
                u = mm(bk[:, :], qo_bf[par][:, 2048 + vfc * 1024 + dc * 128:
                                            2048 + vfc * 1024 + (dc + 1) * 128],
                       oT_sb[:, vfc * 512:(vfc + 1) * 512], vfc == 0, vfc == 1, waits=waits)
            use_region(REG_QO[par])
            dst = x_run[:, dc * 2048 + qt * 512: dc * 2048 + (qt + 1) * 512]
            if h == 0:
                _w0 = [("PE", u), ("ST", wp.get(("stores_done", s), 0))]
                if stage in ("snap", "snapxg"):
                    _w0.append(("SN", P.now("SN")))
                ev = v_op(lambda e, S_, b=bk, d=dst: e.tensor_copy(d, b[:, :]),
                          waits=_w0, inc="DVE")
            else:
                ev = v_op(lambda e, S_, b=bk, d=dst: e.tensor_tensor(d, d, b[:, :], op=ALU.add),
                          waits=[("PE", u)], inc="DVE")
            wp[("wo_evac", s, ug, dc)] = ev

    def residual(s):
        for tt in range(TTN):
            w = [("PE", wp[("attn_done_pe", s)]),
                 ("DVE", wp[("wo_evac", s, H * TTN - 1, C - 1)])]
            if tt > 0:
                w.append(("DVE", wp[("res_consumed", s, tt - 1)]))
            ldv = dma(x_ring, x_img[s, tt], "XL", waits=w)
            for c in range(C):
                dst = x_run[:, c * 2048 + tt * 512: c * 2048 + (tt + 1) * 512]
                v_op(lambda e, S_, d=dst, c_=c: e.tensor_tensor(
                    d, d, x_ring[:, c_ * 512:(c_ + 1) * 512], op=ALU.add),
                    waits=[("XL", ldv)], inc="DVE" if c == C - 1 else None)
            wp[("res_consumed", s, tt)] = P.now("DVE")
        wp[("xmid_done", s)] = P.now("DVE")

    def stream_ffn_winsw(s, e_idx, g):
        par = (e_idx * NG + g) % 2
        cv = None
        for u in range(2):
            cv = stream_unit(ffn_pack[e_idx, g, u],
                             win_r[par][:, u * 2048:(u + 1) * 2048], [REG_WIN[par]])
        for u in range(2):
            cv = stream_unit(ffn_pack[e_idx, g, 2 + u],
                             wsw_r[par][:, u * 2048:(u + 1) * 2048], [REG_WSW[par]])
        wp[("ffn_w", s, e_idx, g)] = cv

    def stream_ffn_wout(s, e_idx, g):
        cv = None
        for u in range(2):
            cv = stream_unit(ffn_pack[e_idx, g, 4 + u],
                             wout_r[:, u * 2048:(u + 1) * 2048], ["wout"])
        wp[("ffn_wout", s, e_idx, g)] = cv

    def ffn_expert_groups(s):
        experts = list(range(NE)) if ffn_experts is None else list(ffn_experts)
        if not experts:
            return
        h_hist = []
        gb_state = {"last": 0}
        units = [(e, g) for e in experts for g in range(NG)]
        tiles = [(e, g, tt) for e in experts for g in range(NG) for tt in range(TTN)]

        def emit_gb(j):
            e_idx, g, tt = tiles[j]
            if e_idx == 0:
                return None
            gbm = mm(bank[6][:, :],
                     sel_sb[0:8, (e_idx - 1) * 128: e_idx * 128],
                     gatesT[0:8, tt * 512:(tt + 1) * 512], True, True,
                     waits=[("DVE", wp[("router_done", s, TTN - 1)]),
                            ("DVE", sel_ld), ("DVE", gb_state["last"])])
            gbc = v_op(lambda e, S_: e.tensor_copy(gb_sb, bank[6][:, :]),
                       waits=[("PE", gbm),
                              ("PE", wp.get(("attnT_region_pe", s), 0))], inc="DVE")
            gb_state["last"] = gbc
            v_drain()
            return gbc

        def emit_xg_all(j, gbc, xg_pe):
            e_idx, g, tt = tiles[j]
            src3 = xlnT[:].rearrange("p (c q) -> p c q", q=2048)[:, :, tt * 512:(tt + 1) * 512]
            dst3 = xg_sb.rearrange("p (c q) -> p c q", c=C)
            gb3 = gb_sb[:, None, :].broadcast_to([128, C, 512])
            v_op(lambda e, S_, s3=src3, d3=dst3, g3=gb3: e.tensor_tensor(
                d3, s3, g3, op=ALU.mult),
                waits=[("DVE", gbc), ("PE", xg_pe)], inc="DVE")
            return P.now("DVE")

        stream_ffn_winsw(s, *units[0])
        stream_ffn_wout(s, *units[0])
        xg_next = None
        for j, (e_idx, g, tt) in enumerate(tiles):
            ui = j // TTN
            par = (e_idx * NG + g) % 2
            if tt == 0 and ui + 1 < len(units):
                stream_ffn_winsw(s, *units[ui + 1])
            wv = wp[("ffn_w", s, e_idx, g)]
            wov = wp[("ffn_wout", s, e_idx, g)]
            if j == 0:
                gbc0 = emit_gb(0)
                if gbc0 is None:
                    xg_next = wp[("router_done", s, TTN - 1)]
                else:
                    xg_next = emit_xg_all(0, gbc0, wp[("slot_lo_pe", s)])
            xg_done = xg_next
            if e_idx > 0:
                rhs_c = [xg_sb[:, c * 512:(c + 1) * 512] for c in range(C)]
            else:
                rhs_c = [xlnT[:, c * 2048 + tt * 512: c * 2048 + (tt + 1) * 512]
                         for c in range(C)]

            for fc in range(4):
                mb = bank[fc % 2]
                sb_ = bank[2 + fc % 2]
                mid_waits0 = [("PL", wv), ("DVE", xg_done)]
                if len(h_hist) >= 2:
                    mid_waits0.append(("DVE", h_hist[-2]))
                for c in range(C):
                    mm(mb[:, :],
                       win_r[par][:, c * 512 + fc * 128: c * 512 + (fc + 1) * 128],
                       rhs_c[c], c == 0, c == C - 1,
                       waits=mid_waits0 if c == 0 else ())
                use_region(REG_WIN[par])
                mid_pe = P.now("PE")
                for c in range(C):
                    mm(sb_[:, :],
                       wsw_r[par][:, c * 512 + fc * 128: c * 512 + (fc + 1) * 128],
                       rhs_c[c], c == 0, c == C - 1,
                       waits=([("DVE", h_hist[-2])] if len(h_hist) >= 2 and c == 0 else ()))
                use_region(REG_WSW[par])
                sw_pe = P.now("PE")
                aw = [("PE", mid_pe)]
                if len(h_hist) >= 1:
                    aw.append(("DVE", h_hist[-1]))
                av = a_op(lambda e, S_, m=mb, f_=fc: e.activation(
                    fftmp[f_ % 2], m[:, :], AF.Silu), waits=aw, inc="ACT")
                hw = [("ACT", av), ("PE", sw_pe)]
                if j == 0 and fc == 0:
                    hw.append(("PE", wp[("slot_lo_pe", s)]))
                if fc == 0:
                    hw.append(("PE", wp.get(("h_read", s), 0)))
                hv = v_op(lambda e, S_, s2=sb_, f_=fc: e.tensor_tensor(
                    h_sb[:, f_ * 512:(f_ + 1) * 512], fftmp[f_ % 2], s2[:, :],
                    op=ALU.mult), waits=hw, inc="DVE")
                h_hist.append(hv)

            xg_pe_now = P.now("PE")
            nxt_gbc = None
            if j + 1 < len(tiles):
                nxt_gbc = emit_gb(j + 1)
                if nxt_gbc is None:
                    xg_next = wp[("router_done", s, TTN - 1)]
                else:
                    xg_next = emit_xg_all(j + 1, nxt_gbc, xg_pe_now)

            for dc in range(C):
                ob = bank[4 + dc % 2]
                for fc in range(4):
                    waits = []
                    if fc == 0:
                        waits = [("PL", wov), ("DVE", h_hist[-1]),
                                 ("DVE", wp.get(("out_evac", s, e_idx, g, tt, dc - 2), 0))]
                        if j == 0 and dc == 0:
                            waits.append(("DVE", wp[("xmid_done", s)]))
                        if dc == 0:
                            waits.append(("DVE", wp.get(("out_prev", s), 0)))
                    mm(ob[:, :],
                       wout_r[:, fc * 1024 + dc * 128: fc * 1024 + (dc + 1) * 128],
                       h_sb[:, fc * 512:(fc + 1) * 512], fc == 0, fc == 3, waits=waits)
                use_region("wout")
                u = P.now("PE")
                dst = x_run[:, dc * 2048 + tt * 512: dc * 2048 + (tt + 1) * 512]
                sw_ = [("PE", u)]
                if stage == "snap":
                    sw_.append(("SN", P.now("SN")))
                ev = v_op(lambda e, S_, b=ob, d=dst: e.tensor_tensor(
                    d, d, b[:, :], op=ALU.add), waits=sw_, inc="DVE")
                wp[("out_evac", s, e_idx, g, tt, dc)] = ev

            wp[("out_prev", s)] = wp[("out_evac", s, e_idx, g, tt, C - 1)]
            if s == 0:
                snap_tile0("u%dtt%d" % (ui, tt))
            wp[("h_read", s)] = P.now("PE")
            if tt == TTN - 1 and ui + 1 < len(units):
                stream_ffn_wout(s, *units[ui + 1])

    def store_tile(s, tt, wait_dve):
        src = x_run[:].rearrange("p (c q) -> p c q", c=C)[:, :, tt * 512:(tt + 1) * 512]
        dst = out_img[s, tt].rearrange("p (c q) -> p c q", c=C)
        return dma(dst, src, "ST", waits=[("DVE", wait_dve)])

    for s in range(S):
        ln_tiles(s, "hbm")
        if stage == "ln1":
            for tt in range(TTN):
                for c in range(C):
                    v_op(lambda e, S_, c_=c, tt_=tt: e.tensor_copy(
                        x_run[:, c_ * 2048 + tt_ * 512: c_ * 2048 + (tt_ + 1) * 512],
                        xlnT[:, c_ * 2048 + tt_ * 512: c_ * 2048 + (tt_ + 1) * 512]),
                        waits=[("DVE", wp[("ln_apply", s, "hbm", tt)])],
                        inc="DVE" if c == C - 1 else None)
                store_tile(s, tt, P.now("DVE"))
            wp[("xring_free_dve", s + 1)] = P.now("DVE")
            wp[("xring_free_pe", s + 1)] = P.now("PE")
            wp[("seq_prev_dve", s + 1)] = P.now("DVE")
            wp[("seq_prev_act", s + 1)] = P.now("ACT")
            wp[("stores_done", s + 1)] = P.now("ST")
            continue
        stream_head(s, 0)
        for h in range(H):
            if h + 1 < H:
                stream_head(s, h + 1)
            head_setup(s, h)
            for qt in range(TTN):
                attn_unit(s, h, qt)
        wp[("attn_done_pe", s)] = P.now("PE")
        residual(s)
        if stage == "attn":
            for tt in range(TTN):
                store_tile(s, tt, wp[("xmid_done", s)])
            wp[("xring_free_dve", s + 1)] = P.now("DVE")
            wp[("xring_free_pe", s + 1)] = P.now("PE")
            wp[("seq_prev_dve", s + 1)] = P.now("DVE")
            wp[("seq_prev_act", s + 1)] = P.now("ACT")
            wp[("stores_done", s + 1)] = P.now("ST")
            continue
        ln_tiles(s, "xrun")
        if stage == "gates":
            gd = v_op(lambda e, S_: e.tensor_copy(x_run[0:8, 0:2048], gatesT[0:8, :]),
                      waits=[("DVE", wp[("router_done", s, TTN - 1)])], inc="DVE")
            P.emit("sync", lambda e, S_, s_=s: e.dma_start(
                out_img[s_, 0][0:8, 0:2048], x_run[0:8, 0:2048]),
                waits=[("DVE", gd)], inc="ST", inc_n=16)
            wp[("xring_free_dve", s + 1)] = P.now("DVE")
            wp[("xring_free_pe", s + 1)] = P.now("PE")
            wp[("seq_prev_dve", s + 1)] = P.now("DVE")
            wp[("seq_prev_act", s + 1)] = P.now("ACT")
            wp[("stores_done", s + 1)] = P.now("ST")
            continue
        wp[("slot_lo_pe", s)] = P.now("PE")
        wp[("attnT_region_pe", s)] = P.now("PE")
        ffn_expert_groups(s)
        _exps = list(range(NE)) if ffn_experts is None else list(ffn_experts)
        if stage != "xg" and not stage.startswith("hdump") and not stage.startswith("odump"):
            for tt in range(TTN):
                if _exps:
                    store_tile(s, tt, wp[("out_evac", s, _exps[-1], NG - 1, tt, C - 1)])
                else:
                    store_tile(s, tt, wp[("xmid_done", s)])
        wp[("xring_free_dve", s + 1)] = P.now("DVE")
        wp[("xring_free_pe", s + 1)] = P.now("PE")
        wp[("seq_prev_dve", s + 1)] = P.now("DVE")
        wp[("seq_prev_act", s + 1)] = P.now("ACT")
        wp[("stores_done", s + 1)] = P.now("ST")

    final_st = P.now("ST")
    final_sn = P.now("SN")
    def _fin(e, S_):
        if final_sn:
            e.wait_ge(S_["SN"], final_sn)
        return e.wait_ge(S_["ST"], final_st)
    P.emit("sync", _fin)

    es = ExitStack()
    sems = {}
    for name in ("WL", "WL0", "WL1", "XL", "ST", "PE", "DVE", "ACT", "PL", "SN"):
        sems[name] = es.enter_context(nc.semaphore(name))
    blk = es.enter_context(nc.Block())

    def mk(eng_name):
        def body(eng):
            for fn in P.q[eng_name]:
                fn(eng, sems)
        return body
    blk.tensor(mk("tensor"))
    blk.vector(mk("vector"))
    blk.scalar(mk("scalar"))
    blk.gpsimd(mk("gpsimd"))
    blk.sync(mk("sync"))
    es.close()
    return nc


def _np32(a):
    a = np.asarray(a)
    return a.astype(np.float32) if a.dtype != np.float32 else a


def _pack_weights(Wq, Wkv_down, Wk_up, Wv_up, Wo, router, w_in_sh, w_out_sh, w_sw_sh,
                  w_in_e, w_sw_e, w_out_e):
    f32 = np.float32
    Wq4 = _np32(Wq).reshape(H, HD, C, 128)
    qo0 = np.transpose(Wq4, (0, 3, 2, 1)).reshape(H, 128, 2048)
    Wo4 = _np32(Wo).reshape(D, H, 2, 128)
    qo1 = np.transpose(Wo4, (1, 3, 2, 0)).reshape(H, 128, 2048)
    qo_pack = np.ascontiguousarray(np.stack([qo0, qo1], axis=1))
    kvk = _np32(Wk_up).reshape(H, HD, LAT).transpose(0, 2, 1)
    kvv = _np32(Wv_up).reshape(H, HD, LAT).transpose(0, 2, 1)
    kv_pack = np.ascontiguousarray(np.concatenate([kvk, kvv], axis=2))
    wk4 = _np32(Wkv_down).reshape(LAT, C, 128).transpose(2, 1, 0).reshape(128, 512)
    r4 = _np32(router).reshape(C, 128, 8).transpose(1, 0, 2).reshape(128, 64)
    wr_pack = np.ascontiguousarray(np.concatenate([wk4, r4], axis=1))
    Win_all = np.concatenate([_np32(w_in_sh).T[None], _np32(w_in_e)], axis=0)
    Wsw_all = np.concatenate([_np32(w_sw_sh).T[None], _np32(w_sw_e)], axis=0)
    Wout_all = np.concatenate([_np32(w_out_sh).T[None], _np32(w_out_e)], axis=0)
    ffn = np.empty((NE, NG, 6, 128, 2048), f32)
    for e in range(NE):
        wi = Win_all[e].reshape(C, 128, NG, 512)
        ws = Wsw_all[e].reshape(C, 128, NG, 512)
        for g in range(NG):
            for u in range(2):
                ffn[e, g, u] = np.transpose(wi[4 * u:4 * u + 4, :, g, :], (1, 0, 2)).reshape(128, 2048)
                ffn[e, g, 2 + u] = np.transpose(ws[4 * u:4 * u + 4, :, g, :], (1, 0, 2)).reshape(128, 2048)
            wo = Wout_all[e][g * 512:(g + 1) * 512].reshape(4, 128, 1024)
            for u in range(2):
                ffn[e, g, 4 + u] = np.transpose(wo[2 * u:2 * u + 2], (1, 0, 2)).reshape(128, 2048)
    return qo_pack, kv_pack, wr_pack, np.ascontiguousarray(ffn)


def _pack_x(x_core):
    arr = _np32(x_core).reshape(S, TTN, TW, C, 128)
    return np.ascontiguousarray(np.transpose(arr, (0, 1, 4, 3, 2)).reshape(S, TTN, 128, C * TW))


def _unpack_out(o):
    arr = np.asarray(o, np.float32).reshape(S, TTN, 128, C, TW)
    return np.transpose(arr, (0, 1, 4, 3, 2)).reshape(S, T, D)


_nc_cache = {}


def kernel(L0_attn_x, L0_attn_Wq, L0_attn_Wkv_down, L0_attn_Wk_up, L0_attn_Wv_up,
           L0_attn_Wo, L0_ffn_router_DE, L0_ffn_w_in_shared_FD, L0_ffn_w_out_shared_DF,
           L0_ffn_w_swiglu_FD, L0_ffn_w_in_eDF, L0_ffn_w_swiglu_eDF, L0_ffn_w_out_eFD,
           stage="full", ffn_experts=None):
    qo_pack, kv_pack, wr_pack, ffn_pack = _pack_weights(
        L0_attn_Wq, L0_attn_Wkv_down, L0_attn_Wk_up, L0_attn_Wv_up, L0_attn_Wo,
        L0_ffn_router_DE, L0_ffn_w_in_shared_FD, L0_ffn_w_out_shared_DF,
        L0_ffn_w_swiglu_FD, L0_ffn_w_in_eDF, L0_ffn_w_swiglu_eDF, L0_ffn_w_out_eFD)
    ident = np.eye(128, dtype=np.float32)
    sel8 = np.zeros((8, 1024), dtype=np.float32)
    for e in range(8):
        sel8[e, e * 128:(e + 1) * 128] = 1.0
    x = _np32(L0_attn_x)

    key = (stage, tuple(ffn_experts) if ffn_experts is not None else None)
    if key not in _nc_cache:
        _nc_cache[key] = build_program(stage, ffn_experts)
    nc = _nc_cache[key]

    in_maps = []
    for c in range(NCORES):
        in_maps.append({
            "x_img": _pack_x(x[S * c:S * (c + 1)]),
            "qo_pack": qo_pack, "kv_pack": kv_pack, "wr_pack": wr_pack,
            "ffn_pack": ffn_pack, "ident": ident, "sel8": sel8,
        })
    res = run_bass_kernel_spmd(nc, in_maps, core_ids=list(range(NCORES)))
    if stage in ("gates", "xg", "snap", "snapxg") or stage.startswith("hdump") or stage.startswith("odump"):
        return [np.asarray(r["out_img"]) for r in res.results]
    outs = [_unpack_out(r["out_img"]) for r in res.results]
    return np.concatenate(outs, axis=0)


# version 12 (final)
# speedup vs baseline: 1.0336x; 1.0261x over previous
import os
import numpy as np
from contextlib import ExitStack

import concourse.bass as bass
import concourse.mybir as mybir
from concourse.bass_utils import run_bass_kernel_spmd

F32 = mybir.dt.float32
BF16 = mybir.dt.bfloat16
U8 = mybir.dt.uint8
AF = mybir.ActivationFunctionType
ALU = mybir.AluOpType
AX = mybir.AxisListType

B, T, D, H, HD, LAT = 16, 2048, 1024, 8, 256, 64
NCORES = 8
S = 2
C = 8
TTN = 4
TW = 512
NE = 9
NG = 4
EPS = 1e-5
BIG = 1.0e5
KB = 1024


PARANOID = [False]
DRAIN_ALL = [os.environ.get("BASS_SIM_FP64") == "1"]


class Plan:
    ENGS = ("tensor", "vector", "scalar", "gpsimd", "sync")

    def __init__(self):
        self.q = {e: [] for e in self.ENGS}
        self.cnt = {}
        self.last_wait = {e: {} for e in self.ENGS}

    def sem(self, name):
        self.cnt[name] = 0

    def now(self, name):
        return self.cnt[name]

    def emit(self, eng, fn, waits=(), inc=None, inc_n=1):
        lw = self.last_wait[eng]
        eff = []
        if PARANOID[0]:
            waits = list(waits) + [(k, self.cnt[k]) for k in self.cnt]
        for s_, v in waits:
            if v <= 0 or lw.get(s_, -1) >= v:
                continue
            lw[s_] = v
            eff.append((s_, v))
        if inc is not None:
            self.cnt[inc] += inc_n

        da = DRAIN_ALL[0] and eng in ("vector", "scalar", "gpsimd")

        def run(e, sems, _eff=tuple(eff), _fn=fn, _inc=inc, _n=inc_n, _da=da):
            for s_, v in _eff:
                e.wait_ge(sems[s_], v)
            if _da:
                e.drain()
            inst = _fn(e, sems)
            if _inc is not None:
                inst.then_inc(sems[_inc], _n)
        self.q[eng].append(run)
        return self.cnt[inc] if inc is not None else None


def build_program(stage="full", ffn_experts=None):
    nc = bass.Bass("TRN2", target_bir_lowering=False, debug=False)

    x_img = nc.dram_tensor("x_img", [S, TTN, 128, C * TW], F32, kind="ExternalInput").ap()
    qo_pack = nc.dram_tensor("qo_pack", [H, 2, 128, 2048], F32, kind="ExternalInput").ap()
    kv_pack = nc.dram_tensor("kv_pack", [H, 64, 512], F32, kind="ExternalInput").ap()
    wr_pack = nc.dram_tensor("wr_pack", [128, 576], F32, kind="ExternalInput").ap()
    ffn_pack = nc.dram_tensor("ffn_pack", [NE, NG, 6, 128, 2048], F32, kind="ExternalInput").ap()
    ident_in = nc.dram_tensor("ident", [128, 128], F32, kind="ExternalInput").ap()
    sel8_in = nc.dram_tensor("sel8", [8, 1024], F32, kind="ExternalInput").ap()
    out_img = nc.dram_tensor("out_img", [S, TTN, 128, C * TW], F32, kind="ExternalOutput").ap()
    snaps = nc.dram_tensor("snaps", [16, 128, C * TW], F32, kind="ExternalOutput").ap() \
        if stage in ("snap", "snapxg") else None

    al = nc.alloc_sbuf_tensor
    x_run = al("x_run", [128, C * 2048], F32)
    xlnT = al("xlnT", [128, C * 2048], BF16)
    lg = al("lg", [128, 4 * KB], U8)
    latT = lg.bitcast(BF16)[0:64, 0:2048]
    gatesT = lg.bitcast(BF16)[0:8, 0:2048]

    slot = al("slot", [128, 30 * KB], U8)
    slot_b = slot.bitcast(BF16)
    slot_f = slot.bitcast(F32)
    kT_sb = slot_b[:, 0:4096]
    V_sb = slot_b[:, 4096:8192]
    qT_sb = slot_b[:, 8192:9216]
    attnT = [slot_b[:, 9216 + i * 512: 9216 + (i + 1) * 512] for i in range(4)]
    oT_sb = slot_b[:, 11264:12288]
    rb_sb = slot_f[:, 6144:6656]
    qsq_sb = slot_f[:, 6656:7168]
    ksq_sb = slot_f[:, 7168:7680]
    x_ring = slot_f[:, 0:4096]
    xln2f = slot_f[:, 0:4096]
    xg_sb = slot_b[:, 0:4096]
    h_sb = slot_b[:, 4096:6144]
    fftmp = [slot_f[:, 6144 + i * 512: 6144 + (i + 1) * 512] for i in range(2)]
    gb_sb = slot_b[:, 10240:10752]

    war = al("war", [128, 56 * KB], U8)
    war_f = war.bitcast(F32)
    war_b = war.bitcast(BF16)
    stage_f2 = [war_f[:, 0:2048], war_f[:, 2048:4096]]
    win_r = {0: war_b[:, 8192:12288], 1: war_b[:, 16384:20480]}
    wsw_r = {0: war_b[:, 12288:16384], 1: war_b[:, 20480:24576]}
    wout_r = war_b[:, 24576:28672]
    qo_bf = {0: war_b[:, 16384:20480], 1: war_b[:, 20480:24576]}
    REG_WIN = {0: "A_lo", 1: "B_lo"}
    REG_WSW = {0: "A_hi", 1: "B_hi"}
    REG_QO = {0: "B_lo", 1: "B_hi"}

    kvbuf = al("kvbuf", [64, 1024], BF16)
    wr_f = al("wr_f", [128, 576], F32)
    wkvd_bf = al("wkvd_bf", [128, 512], BF16)
    ident_f = al("ident_f", [128, 128], F32)
    sel_sb = al("sel_sb", [8, 1024], BF16)
    rows = al("rows", [1, 2564], F32)
    r_srow = rows[0:1, 0:512]
    r_ssrow = rows[0:1, 512:1024]
    r_mrow = rows[0:1, 1024:1536]
    r_m2row = rows[0:1, 1536:2048]
    r_ksq = rows[0:1, 0:2048]
    r_rstd = rows[0:1, 2048:2560]
    r_negq = rows[0:1, 2560:2561]
    r_negk = rows[0:1, 2561:2562]
    r_negs = rows[0:1, 2562:2563]
    ones_f = al("ones_f", [128, 1], F32)
    ones_b = al("ones_b", [128, 1], BF16)
    crow_1 = al("crow_1", [1, 128], F32)
    crow_1b = al("crow_1b", [1, 128], BF16)
    crow_n1024 = al("crow_n1024", [1, 128], F32)
    crow_1_32 = al("crow_1_32", [1, 128], F32)
    bias_sb = al("bias_sb", [128, 2], F32)
    sc_sb = al("sc_sb", [128, 8], F32)
    sg_sb = al("sg_sb", [128, 8], F32)
    ind_sb = al("ind_sb", [128, 8], F32)
    sx_sb = al("sx_sb", [128, 8], F32)
    gtm_sb = al("gtm_sb", [128, 8], F32)
    m1_sb = al("m1_sb", [128, 1], F32)
    m2_sb = al("m2_sb", [128, 1], F32)
    rrow_sb = rows[0:1, 2048:2560]

    bank = [nc.alloc_psum_tensor("bank%d" % i, [128, 512], F32) for i in range(8)]

    P = Plan()
    for s_ in ("WL", "WL0", "WL1", "XL", "ST", "PE", "DVE", "ACT", "PL", "SN"):
        P.sem(s_)
    snap_i = [0]

    def snap_buf(src_ap2, dst_rearr=True):
        if snap_i[0] >= 16:
            return
        i = snap_i[0]
        snap_i[0] += 1
        dst_ap = snaps[i][0:src_ap2.shape[0], 0:src_ap2.shape[1]]
        P.emit("sync", lambda e, S_, s2=src_ap2, d2=dst_ap: e.dma_start(d2, s2),
               waits=[("DVE", P.now("DVE"))], inc="SN", inc_n=16)

    def snap_tile0(tag):
        if stage != "snap" or snap_i[0] >= 16:
            return
        i = snap_i[0]
        snap_i[0] += 1
        src_ap = x_run[:].rearrange("p (c q) -> p c q", c=C)[:, :, 0:512]
        dst_ap = snaps[i].rearrange("p (c q) -> p c q", c=C)
        P.emit("sync", lambda e, S_, s2=src_ap, d2=dst_ap: e.dma_start(d2, s2),
               waits=[("DVE", P.now("DVE"))], inc="SN", inc_n=16)
    wp = {}
    region_pe = {}

    def use_region(name):
        region_pe[name] = P.now("PE")

    def dma(out_ap, in_ap, sem, waits=()):
        return P.emit("sync", lambda e, S_, o=out_ap, i=in_ap: e.dma_start(o, i),
                      waits=waits, inc=sem, inc_n=16)

    def mm(out_ap, lhsT, rhs, start, stop, waits=(), force_inc=False):
        return P.emit("tensor",
                      lambda e, S_, o=out_ap, l=lhsT, r=rhs, st=start, sp=stop:
                      e.matmul(o, l, r, start=st, stop=sp),
                      waits=waits, inc="PE" if (stop or force_inc) else None)

    def v_op(fn, waits=(), inc=None):
        return P.emit("vector", fn, waits=waits, inc=inc)

    def a_op(fn, waits=(), inc=None):
        return P.emit("scalar", fn, waits=waits, inc=inc)

    def v_drain():
        P.emit("vector", lambda e, S_: e.drain())

    st_state = {"i": 0, "cast": {0: 0, 1: 0}}

    def stream_unit(src_ap, dest_ap, regions, nrow=128):
        par = st_state["i"] % 2
        st_state["i"] += 1
        wsem = "WL%d" % par
        st = stage_f2[par] if nrow == 128 else stage_f2[par][0:nrow, 0:512]
        dv = dma(st, src_ap, wsem, waits=[("PL", st_state["cast"][par])])
        waits = [(wsem, dv)] + [("PE", region_pe.get(r, 0)) for r in regions]
        cv = P.emit("gpsimd", lambda e, S_, o=dest_ap, i=st: e.tensor_copy(o, i),
                    waits=waits, inc="PL")
        st_state["cast"][par] = cv
        return cv

    v_op(lambda e, S_: e.memset(ones_f[:], 1.0), inc="DVE")
    v_op(lambda e, S_: e.memset(ones_b[:], 1.0), inc="DVE")
    v_op(lambda e, S_: e.memset(crow_1[:], 1.0), inc="DVE")
    v_op(lambda e, S_: e.memset(crow_1b[:], 1.0), inc="DVE")
    v_op(lambda e, S_: e.memset(crow_n1024[:], -1.0 / 1024.0), inc="DVE")
    v_op(lambda e, S_: e.memset(crow_1_32[:], 1.0 / 32.0), inc="DVE")
    consts_done = P.now("DVE")
    dma(ident_f[:], ident_in, "WL")
    dma(wr_f[:], wr_pack, "WL")
    sel_f = stage_f2[0][0:8, 0:1024]
    preload_ld = dma(sel_f, sel8_in, "WL")
    ident_ld = wr_ld = sel_ld0 = preload_ld
    sel_ld = v_op(lambda e, S_: e.tensor_copy(sel_sb[:], sel_f),
                  waits=[("WL", sel_ld0)], inc="DVE")
    sel_pl = P.emit("gpsimd", lambda e, S_: e.engine_nop(), waits=[("DVE", sel_ld)], inc="PL")
    st_state["cast"][0] = sel_pl
    wkvd_done = v_op(lambda e, S_: e.tensor_copy(wkvd_bf[:], wr_f[:, 0:512]),
                     waits=[("WL", wr_ld)], inc="DVE")

    def ln_tiles(s, kind):
        for tt in range(TTN):
            if kind == "hbm":
                w = [("DVE", wp.get(("xring_free_dve", s), 0)),
                     ("PE", wp.get(("xring_free_pe", s), 0))]
                if tt > 0:
                    w.append(("DVE", wp[("xring_consumed", s, tt - 1)]))
                if stage in ("snap", "snapxg"):
                    w.append(("SN", P.now("SN")))
                ldv = dma(x_ring, x_img[s, tt], "XL", waits=w)
                xs = [x_ring[:, c * 512:(c + 1) * 512] for c in range(C)]
                base = [("XL", ldv), ("DVE", wp.get(("seq_prev_dve", s), 0))]
            else:
                xs = [x_run[:, c * 2048 + tt * 512: c * 2048 + (tt + 1) * 512] for c in range(C)]
                base = [("DVE", wp[("xmid_done", s)]), ("PE", wp[("attn_done_pe", s)])]

            for c in range(C):
                waits = list(base)
                if c == 0:
                    waits += [("DVE", consts_done),
                              ("DVE", wp.get(("ln_apply", s, kind, tt - 1), 0)),
                              ("DVE", wp.get(("lat_evac", s, tt - 1), 0) if kind == "hbm" else
                               wp.get(("router_done", s, tt - 1), 0))]
                mm(bank[4][0:1, :], ones_f[:], xs[c], c == 0, c == C - 1, waits=waits)
            sum_pe = P.now("PE")
            for c in range(C):
                sq = qsq_sb if c % 2 == 0 else ksq_sb
                aw = list(base) + [("PE", wp.get(("ln_sq_used", s, kind, tt, c - 2), 0))]
                av = a_op(lambda e, S_, o=sq, i=xs[c]: e.activation(o, i, AF.Square),
                          waits=aw, inc="ACT")
                mm(bank[5][0:1, :], ones_f[:], sq, c == 0, c == C - 1, waits=[("ACT", av)],
                   force_inc=True)
                wp[("ln_sq_used", s, kind, tt, c)] = P.now("PE")
            ssum_pe = P.now("PE")

            v_op(lambda e, S_: e.tensor_copy(r_srow, bank[4][0:1, :]),
                 waits=[("PE", sum_pe)], inc="DVE")
            v_op(lambda e, S_: e.tensor_copy(r_ssrow, bank[5][0:1, :]),
                 waits=[("PE", ssum_pe)], inc="DVE")
            v_drain()
            v_op(lambda e, S_: e.tensor_scalar(r_mrow, r_srow, 1.0 / 1024.0, None, op0=ALU.mult))
            v_drain()
            v_op(lambda e, S_: e.tensor_tensor(r_m2row, r_mrow, r_mrow, op=ALU.mult))
            v_op(lambda e, S_: e.tensor_scalar(r_ssrow, r_ssrow, 1.0 / 1024.0, None, op0=ALU.mult))
            v_drain()
            v_op(lambda e, S_: e.tensor_tensor(r_ssrow, r_ssrow, r_m2row, op=ALU.subtract))
            v_drain()
            var_done = v_op(lambda e, S_: e.tensor_scalar(r_ssrow, r_ssrow, EPS, None, op0=ALU.add),
                            inc="DVE")
            lnv = a_op(lambda e, S_: e.activation(r_rstd, r_ssrow, AF.Ln),
                       waits=[("DVE", var_done), ("PE", wp.get(("rs_mm", s), 0))], inc="ACT")
            P.emit("scalar", lambda e, S_: e.drain())
            rstd_done = a_op(lambda e, S_: e.activation(r_rstd, r_rstd, AF.Exp, scale=-0.5),
                             waits=[("ACT", lnv)], inc="ACT")
            srow_dve = P.now("DVE")
            mm(bank[6][:, :], crow_n1024[:], r_srow, True, True, waits=[("DVE", srow_dve)])
            rs = mm(bank[7][:, :], crow_1[:], r_rstd, True, True, waits=[("ACT", rstd_done)])
            wp[("rs_mm", s)] = rs
            def _emit_t(c):
                w0 = [("PE", rs)]
                if kind == "xrun" and c == 0:
                    w0.append(("PE", wp.get(("router_pe", s, tt - 1), 0)))
                v_op(lambda e, S_, x_=xs[c], f_=fftmp[c % 2]: e.tensor_tensor(
                    f_, x_, bank[6][:, :], op=ALU.add), waits=w0)

            def _emit_out(c):
                dst = xlnT[:, c * 2048 + tt * 512: c * 2048 + (tt + 1) * 512]
                v_op(lambda e, S_, d=dst, f_=fftmp[c % 2]: e.tensor_tensor(
                    d, f_, bank[7][:, :], op=ALU.mult), inc="DVE")
                if kind == "xrun":
                    v_op(lambda e, S_, c_=c, f_=fftmp[c % 2]: e.tensor_tensor(
                        xln2f[:, c_ * 512:(c_ + 1) * 512], f_, bank[7][:, :], op=ALU.mult),
                        inc="DVE")

            _emit_t(0)
            for c in range(1, C):
                _emit_t(c)
                _emit_out(c - 1)
            _emit_out(C - 1)
            wp[("ln_apply", s, kind, tt)] = P.now("DVE")
            if kind == "hbm":
                wp[("xring_consumed", s, tt)] = P.now("DVE")
                for c in range(C):
                    waits = [("DVE", wp[("ln_apply", s, "hbm", tt)])]
                    if c == 0:
                        waits += [("DVE", wkvd_done), ("DVE", wp.get(("lat_evac", s, tt - 1), 0))]
                    mm(bank[5][0:64, :], wkvd_bf[:, c * 64:(c + 1) * 64],
                       xlnT[:, c * 2048 + tt * 512: c * 2048 + (tt + 1) * 512],
                       c == 0, c == C - 1, waits=waits)
                u = P.now("PE")
                ev = v_op(lambda e, S_, tt_=tt: e.tensor_copy(
                    latT[:, tt_ * 512:(tt_ + 1) * 512], bank[5][0:64, :]),
                    waits=[("PE", u)], inc="DVE")
                wp[("lat_evac", s, tt)] = ev
            else:
                router_tile(s, tt)

    def router_tile(s, tt):
        for tc in range(4):
            for c in range(C):
                waits = [("DVE", wp[("ln_apply", s, "xrun", tt)])]
                if c == 0:
                    waits += [("WL", wr_ld), ("DVE", wp.get(("sc_evac", s, tt, tc - 1), 0))]
                mm(bank[4][:, 0:8], xln2f[:, c * 512 + tc * 128: c * 512 + (tc + 1) * 128],
                   wr_f[:, 512 + c * 8: 512 + (c + 1) * 8], c == 0, c == C - 1, waits=waits)
            u = P.now("PE")
            sc = v_op(lambda e, S_: e.tensor_copy(sc_sb[:], bank[4][:, 0:8]),
                      waits=[("PE", u)], inc="DVE")
            wp[("sc_evac", s, tt, tc)] = sc
            v_drain()
            v_op(lambda e, S_: e.tensor_reduce(m1_sb[:], sc_sb[:], axis=AX.X, op=ALU.max))
            v_drain()
            v_op(lambda e, S_: e.tensor_scalar(ind_sb[:], sc_sb[:], m1_sb[:], None, op0=ALU.is_ge))
            v_drain()
            v_op(lambda e, S_: e.scalar_tensor_tensor(sx_sb[:], ind_sb[:], -BIG, sc_sb[:],
                                                      op0=ALU.mult, op1=ALU.add))
            v_drain()
            v_op(lambda e, S_: e.tensor_reduce(m2_sb[:], sx_sb[:], axis=AX.X, op=ALU.max))
            v_drain()
            v_op(lambda e, S_: e.tensor_scalar(ind_sb[:], sc_sb[:], m2_sb[:], None, op0=ALU.is_ge),
                 inc="DVE")
            av = a_op(lambda e, S_: e.activation(sg_sb[:], sc_sb[:], AF.Exp, scale=-1.0),
                      waits=[("DVE", sc)], inc="ACT")
            v_op(lambda e, S_: e.tensor_scalar(sg_sb[:], sg_sb[:], 1.0, None, op0=ALU.add),
                 waits=[("ACT", av)])
            v_drain()
            v_op(lambda e, S_: e.reciprocal(sg_sb[:], sg_sb[:]))
            v_drain()
            gv = v_op(lambda e, S_: e.tensor_tensor(gtm_sb[:], sg_sb[:], ind_sb[:], op=ALU.mult),
                      inc="DVE")
            tr = P.emit("tensor",
                        lambda e, S_: e.transpose(bank[5][0:8, 0:128], gtm_sb[:], ident_f[:]),
                        waits=[("DVE", gv), ("WL", ident_ld),
                               ("DVE", wp.get(("gT_evac", s, tt, tc - 1), 0))], inc="PE")
            ge = v_op(lambda e, S_, tc_=tc: e.tensor_copy(
                gatesT[0:8, tt * 512 + tc_ * 128: tt * 512 + (tc_ + 1) * 128],
                bank[5][0:8, 0:128]),
                waits=[("PE", tr)], inc="DVE")
            wp[("gT_evac", s, tt, tc)] = ge
        wp[("router_pe", s, tt)] = P.now("PE")
        wp[("router_done", s, tt)] = P.now("DVE")

    def stream_head(s, h):
        par = h % 2
        cv = None
        for u in range(2):
            cv = stream_unit(qo_pack[h, u], qo_bf[par][:, u * 2048:(u + 1) * 2048],
                             [REG_QO[par]])
        wp[("qo_cast", s, h)] = cv
        cv = stream_unit(kv_pack[h], kvbuf[0:64, par * 512:(par + 1) * 512],
                         ["kv%d" % par], nrow=64)
        wp[("kv_cast", s, h)] = cv

    def head_setup(s, h):
        par = h % 2
        i = 0
        for kfc in range(2):
            for tt in range(TTN):
                bk = bank[5 + (i % 2)]
                waits = [("PL", wp[("kv_cast", s, h)]),
                         ("DVE", wp[("lat_evac", s, TTN - 1)]),
                         ("DVE", wp.get(("kT_evac", s, h, i - 2), 0))]
                if i == 0:
                    waits.append(("DVE", wp.get(("wo_evac", s, h * TTN - 1, 7), 0)))
                u = mm(bk[:, :], kvbuf[0:64, par * 512 + kfc * 128: par * 512 + (kfc + 1) * 128],
                       latT[:, tt * 512:(tt + 1) * 512], True, True, waits=waits)
                use_region("kv%d" % par)
                ev = v_op(lambda e, S_, b=bk, kfc_=kfc, tt_=tt: e.tensor_copy(
                    kT_sb[:, kfc_ * 2048 + tt_ * 512: kfc_ * 2048 + (tt_ + 1) * 512], b[:, :]),
                    waits=[("PE", u)], inc="DVE")
                wp[("kT_evac", s, h, i)] = ev
                i += 1
        for tc in range(16):
            bk = bank[5 + (tc % 2)]
            waits = [("DVE", wp.get(("V_evac", s, h, tc - 2), wp[("kT_evac", s, h, 7)]))]
            u = mm(bk[:, 0:256], latT[:, tc * 128:(tc + 1) * 128],
                   kvbuf[0:64, par * 512 + 256: par * 512 + 512], True, True, waits=waits)
            use_region("kv%d" % par)
            ev = v_op(lambda e, S_, b=bk, tc_=tc: e.tensor_copy(
                V_sb[:, tc_ * 256:(tc_ + 1) * 256], b[:, 0:256]),
                waits=[("PE", u)], inc="DVE")
            wp[("V_evac", s, h, tc)] = ev
        for qtr in range(4):
            for dfc in range(2):
                av = a_op(lambda e, S_, dfc_=dfc, q_=qtr: e.activation(
                    ksq_sb, kT_sb[:, dfc_ * 2048 + q_ * 512: dfc_ * 2048 + (q_ + 1) * 512],
                    AF.Square),
                    waits=[("DVE", wp[("kT_evac", s, h, 7)]),
                           ("PE", wp.get(("ksq_used", s, h, qtr, dfc - 1),
                                         wp.get(("ksq_used", s, h, qtr - 1, 1), 0)))],
                    inc="ACT")
                mm(bank[4][0:1, :], ones_f[:], ksq_sb, dfc == 0, dfc == 1,
                   waits=[("ACT", av), ("DVE", wp.get(("wo_evac", s, h * TTN - 1, 7), 0)),
                          ("DVE", wp.get(("ksq_copy", s, h, qtr - 1), 0))],
                   force_inc=True)
                wp[("ksq_used", s, h, qtr, dfc)] = P.now("PE")
            kcv = v_op(lambda e, S_, q_=qtr: e.tensor_copy(
                r_ksq[0:1, q_ * 512:(q_ + 1) * 512], bank[4][0:1, :]),
                waits=[("PE", wp[("ksq_used", s, h, qtr, 1)])], inc="DVE")
            wp[("ksq_copy", s, h, qtr)] = kcv
        v_drain()
        nk = v_op(lambda e, S_: e.tensor_reduce(r_negk, r_ksq, axis=AX.X, op=ALU.max, negate=True),
                  inc="DVE")
        v_drain()
        wp[("negk", s, h)] = nk

    def attn_unit(s, h, qt):
        par = h % 2
        ug = h * TTN + qt
        bias_v = bias_sb[:, h % 2: h % 2 + 1]
        for qfc in range(2):
            bk = bank[5 + qfc]
            for c in range(C):
                waits = [("PL", wp[("qo_cast", s, h)])]
                if c == 0:
                    waits.append(("DVE", wp.get(("qT_evac", s, ug - 1, qfc), 0)))
                    if qt == 0:
                        waits.append(("DVE", wp[("V_evac", s, h, 15)]))
                    if ug == 0:
                        waits.append(("DVE", wp.get(("seq_prev_dve", s), 0)))
                mm(bk[:, :], qo_bf[par][:, c * 256 + qfc * 128: c * 256 + (qfc + 1) * 128],
                   xlnT[:, c * 2048 + qt * 512: c * 2048 + (qt + 1) * 512],
                   c == 0, c == C - 1, waits=waits)
            use_region(REG_QO[par])
            u = P.now("PE")
            ev = v_op(lambda e, S_, b=bk, q_=qfc: e.tensor_copy(
                qT_sb[:, q_ * 512:(q_ + 1) * 512], b[:, :]),
                waits=[("PE", u)], inc="DVE")
            wp[("qT_evac", s, ug, qfc)] = ev
        if qt == 0:
            for qfc in range(2):
                av = a_op(lambda e, S_, q_=qfc: e.activation(
                    qsq_sb, qT_sb[:, q_ * 512:(q_ + 1) * 512], AF.Square),
                    waits=[("DVE", wp[("qT_evac", s, ug, qfc)]),
                           ("PE", wp.get(("nsq_used", s, h, qfc - 1),
                                         wp.get(("nsq_used", s, h - 1, 1), 0)))],
                    inc="ACT")
                mm(bank[4][0:1, :], ones_f[:], qsq_sb, qfc == 0, qfc == 1,
                   waits=[("ACT", av), ("DVE", wp.get(("wo_evac", s, ug - 1, 7), 0))],
                   force_inc=True)
                wp[("nsq_used", s, h, qfc)] = P.now("PE")
            nq = v_op(lambda e, S_: e.tensor_reduce(r_negq, bank[4][0:1, :], axis=AX.X,
                                                    op=ALU.max, negate=True),
                      waits=[("PE", wp[("nsq_used", s, h, 1)]), ("DVE", wp[("negk", s, h)])],
                      inc="DVE")
            v_drain()
            ns = v_op(lambda e, S_: e.tensor_tensor(r_negs, r_negq, r_negk, op=ALU.add),
                      inc="DVE")
            v_drain()
            bm = mm(bank[4][:, 0:1], crow_1_32[:], r_negs, True, True, waits=[("DVE", ns)])
            bc = v_op(lambda e, S_, bv=bias_v: e.tensor_copy(bv, bank[4][:, 0:1]),
                      waits=[("PE", bm), ("ACT", wp.get(("bias_free_h", s, h - 2), 0))],
                      inc="DVE")
            wp[("bias_ready_h", s, h)] = bc
        wp[("bias_ready", s, ug)] = wp[("bias_ready_h", s, h)]

        def scores(kc):
            for dfc in range(2):
                waits = []
                if dfc == 0:
                    waits.append(("ACT", wp.get(("exp", s, ug, kc - 2), 0)))
                    if kc == 0:
                        waits.append(("DVE", wp[("qT_evac", s, ug, 1)]))
                        waits.append(("ACT", wp.get(("exp", s, ug - 1, 15),
                                                    wp.get(("seq_prev_act", s), 0))))
                mm(bank[kc % 2][:, :],
                   kT_sb[:, dfc * 2048 + kc * 128: dfc * 2048 + (kc + 1) * 128],
                   qT_sb[:, dfc * 512:(dfc + 1) * 512], dfc == 0, dfc == 1, waits=waits)
            wp[("scores", s, ug, kc)] = P.now("PE")

        def exp_chunk(kc):
            waits = [("PE", wp[("scores", s, ug, kc)]),
                     ("DVE", wp[("bias_ready", s, ug)]),
                     ("PE", wp.get(("oT_mm", s, ug, kc - 4), 0))]
            av = a_op(lambda e, S_, kc_=kc: e.activation(
                attnT[kc_ % 4], bank[kc_ % 2][:, :], AF.Exp, bias=bias_v, scale=1.0 / 16.0),
                waits=waits, inc="ACT")
            wp[("exp", s, ug, kc)] = av

        def sums_oT(kc):
            ew = wp[("exp", s, ug, kc)]
            w1 = [("ACT", ew)]
            if kc == 0:
                w1.append(("DVE", wp.get(("wo_evac", s, ug - 1, 7), 0)))
            mm(bank[4][0:1, :], ones_b[:], attnT[kc % 4], kc == 0, kc == 15,
               waits=w1)
            if kc == 15:
                wp[("sums_done", s, ug)] = P.now("PE")
            for vfc in range(2):
                w2 = []
                if kc == 0:
                    w2 = [("DVE", wp[("V_evac", s, h, 15)]),
                          ("DVE", wp.get(("oT_evac", s, ug - 1, vfc), 0))]
                mm(bank[2 + vfc][:, :], V_sb[:, kc * 256 + vfc * 128: kc * 256 + (vfc + 1) * 128],
                   attnT[kc % 4], kc == 0, kc == 15, waits=w2, force_inc=(vfc == 1))
            wp[("oT_mm", s, ug, kc)] = P.now("PE")

        for kc in range(16):
            scores(kc)
            exp_chunk(kc)
            if kc >= 1:
                sums_oT(kc - 1)
        sums_oT(15)
        if qt == TTN - 1:
            wp[("bias_free_h", s, h)] = wp[("exp", s, ug, 15)]

        v_op(lambda e, S_: e.tensor_copy(rrow_sb[:], bank[4][0:1, :]),
             waits=[("PE", wp[("sums_done", s, ug)])], inc="DVE")
        v_drain()
        rv = v_op(lambda e, S_: e.reciprocal(rrow_sb[:], rrow_sb[:]), inc="DVE")
        rbm = mm(bank[4][:, :], crow_1[:], rrow_sb[:], True, True, waits=[("DVE", rv)])
        rbc = v_op(lambda e, S_: e.tensor_copy(rb_sb, bank[4][:, :]),
                   waits=[("PE", rbm)], inc="DVE")
        wp[("rbc", s, ug)] = rbc
        for vfc in range(2):
            ev = v_op(lambda e, S_, v_=vfc: e.tensor_tensor(
                oT_sb[:, v_ * 512:(v_ + 1) * 512], bank[2 + v_][:, :], rb_sb, op=ALU.mult),
                waits=[("PE", wp[("oT_mm", s, ug, 15)]), ("DVE", rbc)], inc="DVE")
            wp[("oT_evac", s, ug, vfc)] = ev
        for dc in range(C):
            bk = bank[7 if dc % 2 == 0 else 4]
            for vfc in range(2):
                waits = []
                if vfc == 0:
                    waits.append(("DVE", wp[("oT_evac", s, ug, 1)]))
                    waits.append(("DVE", wp.get(("wo_evac", s, ug, dc - 2), 0)))
                    if dc == 1:
                        waits.append(("DVE", rbc))
                u = mm(bk[:, :], qo_bf[par][:, 2048 + vfc * 1024 + dc * 128:
                                            2048 + vfc * 1024 + (dc + 1) * 128],
                       oT_sb[:, vfc * 512:(vfc + 1) * 512], vfc == 0, vfc == 1, waits=waits)
            use_region(REG_QO[par])
            dst = x_run[:, dc * 2048 + qt * 512: dc * 2048 + (qt + 1) * 512]
            if h == 0:
                _w0 = [("PE", u), ("ST", wp.get(("stores_done", s), 0))]
                if stage in ("snap", "snapxg"):
                    _w0.append(("SN", P.now("SN")))
                ev = v_op(lambda e, S_, b=bk, d=dst: e.tensor_copy(d, b[:, :]),
                          waits=_w0, inc="DVE")
            else:
                ev = v_op(lambda e, S_, b=bk, d=dst: e.tensor_tensor(d, d, b[:, :], op=ALU.add),
                          waits=[("PE", u)], inc="DVE")
            wp[("wo_evac", s, ug, dc)] = ev

    def residual(s):
        for tt in range(TTN):
            w = [("PE", wp[("attn_done_pe", s)]),
                 ("DVE", wp[("wo_evac", s, H * TTN - 1, C - 1)])]
            if tt > 0:
                w.append(("DVE", wp[("res_consumed", s, tt - 1)]))
            ldv = dma(x_ring, x_img[s, tt], "XL", waits=w)
            for c in range(C):
                dst = x_run[:, c * 2048 + tt * 512: c * 2048 + (tt + 1) * 512]
                v_op(lambda e, S_, d=dst, c_=c: e.tensor_tensor(
                    d, d, x_ring[:, c_ * 512:(c_ + 1) * 512], op=ALU.add),
                    waits=[("XL", ldv)], inc="DVE" if c == C - 1 else None)
            wp[("res_consumed", s, tt)] = P.now("DVE")
        wp[("xmid_done", s)] = P.now("DVE")

    def stream_ffn_winsw(s, e_idx, g):
        par = (e_idx * NG + g) % 2
        cv = None
        for u in range(2):
            cv = stream_unit(ffn_pack[e_idx, g, u],
                             win_r[par][:, u * 2048:(u + 1) * 2048], [REG_WIN[par]])
        for u in range(2):
            cv = stream_unit(ffn_pack[e_idx, g, 2 + u],
                             wsw_r[par][:, u * 2048:(u + 1) * 2048], [REG_WSW[par]])
        wp[("ffn_w", s, e_idx, g)] = cv

    def stream_ffn_wout(s, e_idx, g):
        cv = None
        for u in range(2):
            cv = stream_unit(ffn_pack[e_idx, g, 4 + u],
                             wout_r[:, u * 2048:(u + 1) * 2048], ["wout"])
        wp[("ffn_wout", s, e_idx, g)] = cv

    def ffn_expert_groups(s):
        experts = list(range(NE)) if ffn_experts is None else list(ffn_experts)
        if not experts:
            return
        h_hist = []
        gb_state = {"last": 0}
        units = [(e, g) for e in experts for g in range(NG)]
        tiles = [(e, g, tt) for e in experts for g in range(NG) for tt in range(TTN)]

        def emit_gb(j):
            e_idx, g, tt = tiles[j]
            if e_idx == 0:
                return None
            gbm = mm(bank[6][:, :],
                     sel_sb[0:8, (e_idx - 1) * 128: e_idx * 128],
                     gatesT[0:8, tt * 512:(tt + 1) * 512], True, True,
                     waits=[("DVE", wp[("router_done", s, TTN - 1)]),
                            ("DVE", sel_ld), ("DVE", gb_state["last"])])
            gbc = v_op(lambda e, S_: e.tensor_copy(gb_sb, bank[6][:, :]),
                       waits=[("PE", gbm),
                              ("PE", wp.get(("attnT_region_pe", s), 0))], inc="DVE")
            gb_state["last"] = gbc
            v_drain()
            return gbc

        def emit_xg_all(j, gbc, xg_pe):
            e_idx, g, tt = tiles[j]
            src3 = xlnT[:].rearrange("p (c q) -> p c q", q=2048)[:, :, tt * 512:(tt + 1) * 512]
            dst3 = xg_sb.rearrange("p (c q) -> p c q", c=C)
            gb3 = gb_sb[:, None, :].broadcast_to([128, C, 512])
            v_op(lambda e, S_, s3=src3, d3=dst3, g3=gb3: e.tensor_tensor(
                d3, s3, g3, op=ALU.mult),
                waits=[("DVE", gbc), ("PE", xg_pe)], inc="DVE")
            return P.now("DVE")

        stream_ffn_winsw(s, *units[0])
        stream_ffn_wout(s, *units[0])
        xg_next = None
        for j, (e_idx, g, tt) in enumerate(tiles):
            ui = j // TTN
            par = (e_idx * NG + g) % 2
            if tt == 0 and ui + 1 < len(units):
                stream_ffn_winsw(s, *units[ui + 1])
            wv = wp[("ffn_w", s, e_idx, g)]
            wov = wp[("ffn_wout", s, e_idx, g)]
            if j == 0:
                gbc0 = emit_gb(0)
                if gbc0 is None:
                    xg_next = wp[("router_done", s, TTN - 1)]
                else:
                    xg_next = emit_xg_all(0, gbc0, wp[("slot_lo_pe", s)])
            xg_done = xg_next
            if e_idx > 0:
                rhs_c = [xg_sb[:, c * 512:(c + 1) * 512] for c in range(C)]
            else:
                rhs_c = [xlnT[:, c * 2048 + tt * 512: c * 2048 + (tt + 1) * 512]
                         for c in range(C)]

            for fc in range(4):
                mb = bank[fc % 2]
                sb_ = bank[2 + fc % 2]
                mid_waits0 = [("PL", wv), ("DVE", xg_done)]
                if len(h_hist) >= 2:
                    mid_waits0.append(("DVE", h_hist[-2]))
                for c in range(C):
                    mm(mb[:, :],
                       win_r[par][:, c * 512 + fc * 128: c * 512 + (fc + 1) * 128],
                       rhs_c[c], c == 0, c == C - 1,
                       waits=mid_waits0 if c == 0 else ())
                use_region(REG_WIN[par])
                mid_pe = P.now("PE")
                for c in range(C):
                    mm(sb_[:, :],
                       wsw_r[par][:, c * 512 + fc * 128: c * 512 + (fc + 1) * 128],
                       rhs_c[c], c == 0, c == C - 1,
                       waits=([("DVE", h_hist[-2])] if len(h_hist) >= 2 and c == 0 else ()))
                use_region(REG_WSW[par])
                sw_pe = P.now("PE")
                aw = [("PE", mid_pe)]
                if len(h_hist) >= 1:
                    aw.append(("DVE", h_hist[-1]))
                av = a_op(lambda e, S_, m=mb, f_=fc: e.activation(
                    fftmp[f_ % 2], m[:, :], AF.Silu), waits=aw, inc="ACT")
                hw = [("ACT", av), ("PE", sw_pe)]
                if j == 0 and fc == 0:
                    hw.append(("PE", wp[("slot_lo_pe", s)]))
                if fc == 0:
                    hw.append(("PE", wp.get(("h_read", s), 0)))
                hv = v_op(lambda e, S_, s2=sb_, f_=fc: e.tensor_tensor(
                    h_sb[:, f_ * 512:(f_ + 1) * 512], fftmp[f_ % 2], s2[:, :],
                    op=ALU.mult), waits=hw, inc="DVE")
                h_hist.append(hv)

            xg_pe_now = P.now("PE")
            nxt_gbc = None
            if j + 1 < len(tiles):
                nxt_gbc = emit_gb(j + 1)
                if nxt_gbc is None:
                    xg_next = wp[("router_done", s, TTN - 1)]
                else:
                    xg_next = emit_xg_all(j + 1, nxt_gbc, xg_pe_now)

            for dc in range(C):
                ob = bank[4 + dc % 2]
                for fc in range(4):
                    waits = []
                    if fc == 0:
                        waits = [("PL", wov), ("DVE", h_hist[-1]),
                                 ("DVE", wp.get(("out_evac", s, e_idx, g, tt, dc - 2), 0))]
                        if j == 0 and dc == 0:
                            waits.append(("DVE", wp[("xmid_done", s)]))
                        if dc == 0:
                            waits.append(("DVE", wp.get(("out_prev", s), 0)))
                    mm(ob[:, :],
                       wout_r[:, fc * 1024 + dc * 128: fc * 1024 + (dc + 1) * 128],
                       h_sb[:, fc * 512:(fc + 1) * 512], fc == 0, fc == 3, waits=waits)
                use_region("wout")
                u = P.now("PE")
                dst = x_run[:, dc * 2048 + tt * 512: dc * 2048 + (tt + 1) * 512]
                sw_ = [("PE", u)]
                if stage == "snap":
                    sw_.append(("SN", P.now("SN")))
                ev = v_op(lambda e, S_, b=ob, d=dst: e.tensor_tensor(
                    d, d, b[:, :], op=ALU.add), waits=sw_, inc="DVE")
                wp[("out_evac", s, e_idx, g, tt, dc)] = ev

            wp[("out_prev", s)] = wp[("out_evac", s, e_idx, g, tt, C - 1)]
            if s == 0:
                snap_tile0("u%dtt%d" % (ui, tt))
            wp[("h_read", s)] = P.now("PE")
            if tt == TTN - 1 and ui + 1 < len(units):
                stream_ffn_wout(s, *units[ui + 1])

    def store_tile(s, tt, wait_dve):
        src = x_run[:].rearrange("p (c q) -> p c q", c=C)[:, :, tt * 512:(tt + 1) * 512]
        dst = out_img[s, tt].rearrange("p (c q) -> p c q", c=C)
        return dma(dst, src, "ST", waits=[("DVE", wait_dve)])

    for s in range(S):
        ln_tiles(s, "hbm")
        if stage == "ln1":
            for tt in range(TTN):
                for c in range(C):
                    v_op(lambda e, S_, c_=c, tt_=tt: e.tensor_copy(
                        x_run[:, c_ * 2048 + tt_ * 512: c_ * 2048 + (tt_ + 1) * 512],
                        xlnT[:, c_ * 2048 + tt_ * 512: c_ * 2048 + (tt_ + 1) * 512]),
                        waits=[("DVE", wp[("ln_apply", s, "hbm", tt)])],
                        inc="DVE" if c == C - 1 else None)
                store_tile(s, tt, P.now("DVE"))
            wp[("xring_free_dve", s + 1)] = P.now("DVE")
            wp[("xring_free_pe", s + 1)] = P.now("PE")
            wp[("seq_prev_dve", s + 1)] = P.now("DVE")
            wp[("seq_prev_act", s + 1)] = P.now("ACT")
            wp[("stores_done", s + 1)] = P.now("ST")
            continue
        stream_head(s, 0)
        for h in range(H):
            if h + 1 < H:
                stream_head(s, h + 1)
            head_setup(s, h)
            for qt in range(TTN):
                attn_unit(s, h, qt)
        wp[("attn_done_pe", s)] = P.now("PE")
        residual(s)
        if stage == "attn":
            for tt in range(TTN):
                store_tile(s, tt, wp[("xmid_done", s)])
            wp[("xring_free_dve", s + 1)] = P.now("DVE")
            wp[("xring_free_pe", s + 1)] = P.now("PE")
            wp[("seq_prev_dve", s + 1)] = P.now("DVE")
            wp[("seq_prev_act", s + 1)] = P.now("ACT")
            wp[("stores_done", s + 1)] = P.now("ST")
            continue
        ln_tiles(s, "xrun")
        if stage == "gates":
            gd = v_op(lambda e, S_: e.tensor_copy(x_run[0:8, 0:2048], gatesT[0:8, :]),
                      waits=[("DVE", wp[("router_done", s, TTN - 1)])], inc="DVE")
            P.emit("sync", lambda e, S_, s_=s: e.dma_start(
                out_img[s_, 0][0:8, 0:2048], x_run[0:8, 0:2048]),
                waits=[("DVE", gd)], inc="ST", inc_n=16)
            wp[("xring_free_dve", s + 1)] = P.now("DVE")
            wp[("xring_free_pe", s + 1)] = P.now("PE")
            wp[("seq_prev_dve", s + 1)] = P.now("DVE")
            wp[("seq_prev_act", s + 1)] = P.now("ACT")
            wp[("stores_done", s + 1)] = P.now("ST")
            continue
        wp[("slot_lo_pe", s)] = P.now("PE")
        wp[("attnT_region_pe", s)] = P.now("PE")
        ffn_expert_groups(s)
        _exps = list(range(NE)) if ffn_experts is None else list(ffn_experts)
        if stage != "xg" and not stage.startswith("hdump") and not stage.startswith("odump"):
            for tt in range(TTN):
                if _exps:
                    store_tile(s, tt, wp[("out_evac", s, _exps[-1], NG - 1, tt, C - 1)])
                else:
                    store_tile(s, tt, wp[("xmid_done", s)])
        wp[("xring_free_dve", s + 1)] = P.now("DVE")
        wp[("xring_free_pe", s + 1)] = P.now("PE")
        wp[("seq_prev_dve", s + 1)] = P.now("DVE")
        wp[("seq_prev_act", s + 1)] = P.now("ACT")
        wp[("stores_done", s + 1)] = P.now("ST")

    final_st = P.now("ST")
    final_sn = P.now("SN")
    def _fin(e, S_):
        if final_sn:
            e.wait_ge(S_["SN"], final_sn)
        return e.wait_ge(S_["ST"], final_st)
    P.emit("sync", _fin)

    es = ExitStack()
    sems = {}
    for name in ("WL", "WL0", "WL1", "XL", "ST", "PE", "DVE", "ACT", "PL", "SN"):
        sems[name] = es.enter_context(nc.semaphore(name))
    blk = es.enter_context(nc.Block())

    def mk(eng_name):
        def body(eng):
            for fn in P.q[eng_name]:
                fn(eng, sems)
        return body
    blk.tensor(mk("tensor"))
    blk.vector(mk("vector"))
    blk.scalar(mk("scalar"))
    blk.gpsimd(mk("gpsimd"))
    blk.sync(mk("sync"))
    es.close()
    return nc


def _np32(a):
    a = np.asarray(a)
    return a.astype(np.float32) if a.dtype != np.float32 else a


def _pack_weights(Wq, Wkv_down, Wk_up, Wv_up, Wo, router, w_in_sh, w_out_sh, w_sw_sh,
                  w_in_e, w_sw_e, w_out_e):
    f32 = np.float32
    Wq4 = _np32(Wq).reshape(H, HD, C, 128)
    qo0 = np.transpose(Wq4, (0, 3, 2, 1)).reshape(H, 128, 2048)
    Wo4 = _np32(Wo).reshape(D, H, 2, 128)
    qo1 = np.transpose(Wo4, (1, 3, 2, 0)).reshape(H, 128, 2048)
    qo_pack = np.ascontiguousarray(np.stack([qo0, qo1], axis=1))
    kvk = _np32(Wk_up).reshape(H, HD, LAT).transpose(0, 2, 1)
    kvv = _np32(Wv_up).reshape(H, HD, LAT).transpose(0, 2, 1)
    kv_pack = np.ascontiguousarray(np.concatenate([kvk, kvv], axis=2))
    wk4 = _np32(Wkv_down).reshape(LAT, C, 128).transpose(2, 1, 0).reshape(128, 512)
    r4 = _np32(router).reshape(C, 128, 8).transpose(1, 0, 2).reshape(128, 64)
    wr_pack = np.ascontiguousarray(np.concatenate([wk4, r4], axis=1))
    Win_all = np.concatenate([_np32(w_in_sh).T[None], _np32(w_in_e)], axis=0)
    Wsw_all = np.concatenate([_np32(w_sw_sh).T[None], _np32(w_sw_e)], axis=0)
    Wout_all = np.concatenate([_np32(w_out_sh).T[None], _np32(w_out_e)], axis=0)
    ffn = np.empty((NE, NG, 6, 128, 2048), f32)
    for e in range(NE):
        wi = Win_all[e].reshape(C, 128, NG, 512)
        ws = Wsw_all[e].reshape(C, 128, NG, 512)
        for g in range(NG):
            for u in range(2):
                ffn[e, g, u] = np.transpose(wi[4 * u:4 * u + 4, :, g, :], (1, 0, 2)).reshape(128, 2048)
                ffn[e, g, 2 + u] = np.transpose(ws[4 * u:4 * u + 4, :, g, :], (1, 0, 2)).reshape(128, 2048)
            wo = Wout_all[e][g * 512:(g + 1) * 512].reshape(4, 128, 1024)
            for u in range(2):
                ffn[e, g, 4 + u] = np.transpose(wo[2 * u:2 * u + 2], (1, 0, 2)).reshape(128, 2048)
    return qo_pack, kv_pack, wr_pack, np.ascontiguousarray(ffn)


def _pack_x(x_core):
    arr = _np32(x_core).reshape(S, TTN, TW, C, 128)
    return np.ascontiguousarray(np.transpose(arr, (0, 1, 4, 3, 2)).reshape(S, TTN, 128, C * TW))


def _unpack_out(o):
    arr = np.asarray(o, np.float32).reshape(S, TTN, 128, C, TW)
    return np.transpose(arr, (0, 1, 4, 3, 2)).reshape(S, T, D)


_nc_cache = {}


def kernel(L0_attn_x, L0_attn_Wq, L0_attn_Wkv_down, L0_attn_Wk_up, L0_attn_Wv_up,
           L0_attn_Wo, L0_ffn_router_DE, L0_ffn_w_in_shared_FD, L0_ffn_w_out_shared_DF,
           L0_ffn_w_swiglu_FD, L0_ffn_w_in_eDF, L0_ffn_w_swiglu_eDF, L0_ffn_w_out_eFD,
           stage="full", ffn_experts=None):
    qo_pack, kv_pack, wr_pack, ffn_pack = _pack_weights(
        L0_attn_Wq, L0_attn_Wkv_down, L0_attn_Wk_up, L0_attn_Wv_up, L0_attn_Wo,
        L0_ffn_router_DE, L0_ffn_w_in_shared_FD, L0_ffn_w_out_shared_DF,
        L0_ffn_w_swiglu_FD, L0_ffn_w_in_eDF, L0_ffn_w_swiglu_eDF, L0_ffn_w_out_eFD)
    ident = np.eye(128, dtype=np.float32)
    sel8 = np.zeros((8, 1024), dtype=np.float32)
    for e in range(8):
        sel8[e, e * 128:(e + 1) * 128] = 1.0
    x = _np32(L0_attn_x)

    key = (stage, tuple(ffn_experts) if ffn_experts is not None else None)
    if key not in _nc_cache:
        _nc_cache[key] = build_program(stage, ffn_experts)
    nc = _nc_cache[key]

    in_maps = []
    for c in range(NCORES):
        in_maps.append({
            "x_img": _pack_x(x[S * c:S * (c + 1)]),
            "qo_pack": qo_pack, "kv_pack": kv_pack, "wr_pack": wr_pack,
            "ffn_pack": ffn_pack, "ident": ident, "sel8": sel8,
        })
    res = run_bass_kernel_spmd(nc, in_maps, core_ids=list(range(NCORES)))
    if stage in ("gates", "xg", "snap", "snapxg") or stage.startswith("hdump") or stage.startswith("odump"):
        return [np.asarray(r["out_img"]) for r in res.results]
    outs = [_unpack_out(r["out_img"]) for r in res.results]
    return np.concatenate(outs, axis=0)
